# version 01
# baseline (speedup 1.0000x reference)
import numpy as np
import ml_dtypes
from contextlib import ExitStack

import concourse.bass as bass
import concourse.mybir as mybir
from concourse.bass_utils import run_bass_kernel_spmd

F32 = mybir.dt.float32
BF16 = mybir.dt.bfloat16
FP8 = mybir.dt.float8e4
I32 = mybir.dt.int32
AF = mybir.ActivationFunctionType
ALU = mybir.AluOpType
AX = mybir.AxisListType.X

B, T, D, NH, HD, HF = 16, 2048, 1024, 16, 64, 4096
NT, DC, HC = T // 128, D // 128, HF // 128
SEQS = 2
L0_WIN = 4

N_SUBLAYERS = 6
DEBUG_STOP = False
DEBUG_TAPS = False


class _StopEmit(Exception):
    pass


class Sched:
    ENGINES = ("sync", "gpsimd", "tensor", "vector", "scalar")

    def __init__(self, nc, es):
        self.nc = nc
        self.es = es
        self.items = {e: [] for e in self.ENGINES}
        self.sems = {}
        self.cnt = {}

    def sem(self, name):
        if name not in self.sems:
            self.sems[name] = self.es.enter_context(self.nc.semaphore(name))
            self.cnt[name] = 0
        return self.sems[name]

    def emit(self, eng, fn, inc=None, k=1):
        if inc is not None:
            self.sem(inc)
            self.cnt[inc] += k
        self.items[eng].append(("op", fn, inc, k))

    def dma(self, eng, out_ap, in_ap, inc, slow=False):
        self.sem(inc)
        self.cnt[inc] += 16
        self.items[eng].append(("dma", (out_ap, in_ap, slow), inc, 16))

    def wait(self, eng, semname, val=None):
        self.sem(semname)
        if val is None:
            val = self.cnt[semname]
        assert val <= self.cnt[semname]
        if val > 0:
            self.items[eng].append(("wait", semname, val, 0))

    def val(self, semname):
        self.sem(semname)
        return self.cnt[semname]

    def build(self, block):
        def mk(eng_name):
            def run(engine):
                for kind, a, bb, k in self.items[eng_name]:
                    if kind == "wait":
                        engine.wait_ge(self.sems[a], bb)
                    elif kind == "dma":
                        engine.dma_start(out=a[0], in_=a[1],
                                         allow_slow_non_contiguous=a[2]
                                         ).then_inc(self.sems[bb], 16)
                    else:
                        inst = a(engine)
                        if bb is not None:
                            inst.then_inc(self.sems[bb], k)
            return run
        block.sync(mk("sync"))
        block.gpsimd(mk("gpsimd"))
        block.tensor(mk("tensor"))
        block.vector(mk("vector"))
        block.scalar(mk("scalar"))


def _prep(inputs):
    t = lambda a: np.ascontiguousarray(np.asarray(a).T)
    d = {}
    for l, p in ((0, "L0"), (1, "L1")):
        for w, n in (("wqt", "Wq"), ("wkt", "Wk"), ("wvt", "Wv"), ("wot", "Wo")):
            d[f"{w}{l}"] = t(inputs[f"{p}_attn_{n}"])
        for b in ("bq", "bk", "bv", "bo"):
            d[f"{b}{l}"] = np.ascontiguousarray(inputs[f"{p}_attn_{b}"])
    d["w1t0"] = t(inputs["L0_ffn_W1"]); d["w2t0"] = t(inputs["L0_ffn_W2"])
    d["b1_0"] = np.ascontiguousarray(inputs["L0_ffn_b1"])
    d["b2_0"] = np.ascontiguousarray(inputs["L0_ffn_b2"])
    d["w1t1"] = t(inputs["L1_ffn_w1"]); d["w2t1"] = t(inputs["L1_ffn_w2"])
    d["n1w1"] = np.ascontiguousarray(inputs["L1_ffn_norm1_w"])
    d["n2w1"] = np.ascontiguousarray(inputs["L1_ffn_norm2_w"])
    d["asc1"] = np.ascontiguousarray(inputs["L1_ffn_act_scale"]).reshape(1)
    d["abi1"] = np.ascontiguousarray(inputs["L1_ffn_act_bias"]).reshape(1)
    d["ident"] = np.eye(128, dtype=ml_dtypes.bfloat16)
    e2 = np.zeros((128, 2), np.float32)
    e2[0:64, 0] = 1.0
    e2[64:128, 1] = 1.0
    d["e2"] = e2.astype(ml_dtypes.bfloat16)
    d["e2t"] = np.ascontiguousarray(e2.T)
    bj = np.arange(128, dtype=np.float32)
    d["biasv"] = np.ascontiguousarray(
        np.stack([bj - 64.0 - 128.0 * z for z in range(L0_WIN)], axis=1))
    d["tri"] = (np.arange(128)[:, None] <= np.arange(128)[None, :]
                ).astype(ml_dtypes.bfloat16)
    d["epsv"] = np.broadcast_to(np.array([1e-5, 1e-8], np.float32), (128, 2)).copy()
    d["ones1"] = np.ones((1, 128), np.float32)
    d["ones128"] = np.ones((128, 1), np.float32)
    d["onesr"] = np.ones((65, 128), np.float32)
    d["ones128b"] = np.ones((128, 1), ml_dtypes.bfloat16)
    return d


def build_program(n_sub=6, DEBUG_TAPS=False, DEBUG_STOP=False):
    nc = bass.Bass("TRN2", target_bir_lowering=False,
                   detect_race_conditions=False)
    es = ExitStack()
    dram = {}

    def din(name, shape, dtype=F32):
        dram[name] = nc.dram_tensor(name, list(shape), dtype, kind="ExternalInput").ap()

    din("x", [SEQS, T, D])
    for l in range(2):
        for w in ("wqt", "wkt", "wvt", "wot"):
            din(f"{w}{l}", [D, D])
        for b in ("bq", "bk", "bv", "bo"):
            din(f"{b}{l}", [D])
    din("w1t0", [D, HF]); din("w2t0", [HF, D])
    din("b1_0", [HF]); din("b2_0", [D])
    din("w1t1", [D, HF]); din("w2t1", [HF, D])
    din("n1w1", [D]); din("n2w1", [HF])
    din("asc1", [1]); din("abi1", [1])
    din("ident", [128, 128], BF16)
    din("e2", [128, 2], BF16); din("e2t", [2, 128])
    din("biasv", [128, L0_WIN])
    din("tri", [128, 128], BF16)
    din("ones1", [1, 128]); din("ones128", [128, 1]); din("epsv", [128, 2])
    din("onesr", [65, 128]); din("ones128b", [128, 1], BF16)
    x_d = dram["x"]
    y_d = nc.dram_tensor("y", [SEQS, T, D], F32, kind="ExternalOutput").ap()
    taps = {}
    if DEBUG_TAPS:
        for nm, sh, dt in (("t_xq", [2, 128, D], BF16), ("t_h", [2, 128, HF], F32),
                           ("t_hq", [2, 128, HF], BF16), ("t_stt", [2, 128, 16], F32),
                           ("t_w1q", [128, DC * HF], FP8), ("t_w2q", [128, HC * D], FP8),
                           ("t_ws", [128, 4], F32), ("t_m", [2, 128, 1], F32), ("t_b1", [128, HF], F32),
                           ("t_r", [2, 128, 1], F32), ("t_vv", [2, 128, 1], F32)):
            taps[nm] = nc.dram_tensor(nm, sh, dt, kind="ExternalOutput").ap()
    wscr = {}
    for l in range(2):
        wscr[(l, 1)] = nc.dram_tensor(f"w1scr{l}", [D, HF], FP8, kind="Internal").ap()
        wscr[(l, 2)] = nc.dram_tensor(f"w2scr{l}", [HF, D], FP8, kind="Internal").ap()

    sbt = lambda name, shape, dtype=F32: es.enter_context(
        nc.sbuf_tensor(name, list(shape), dtype))
    x_sb = sbt("x_sb", [128, NT * D])
    ident = sbt("ident_sb", [128, 128], BF16)
    e2_sb = sbt("e2_sb", [128, 2], BF16)
    e2t_sb = sbt("e2t_sb", [2, 128])
    biasv_sb = sbt("biasv_sb", [128, L0_WIN])
    tri_sb = sbt("tri_sb", [128, 128], BF16)
    ones1_sb = sbt("ones1_sb", [1, 128])
    eps_sb = sbt("eps_sb", [128, 2])
    ones128_sb = sbt("ones128_sb", [128, 1])
    onesr_sb = sbt("onesr_sb", [65, 128])
    ones128b_sb = sbt("ones128b_sb", [128, 1], BF16)
    ws_sb = sbt("ws_sb", [128, 4])
    ai_sb = sbt("ai_sb", [128, 2])
    st_m = sbt("st_m", [128, NT])
    st_r = sbt("st_r", [128, NT])
    st_r2 = sbt("st_r2", [128, NT])
    st_v = sbt("st_v", [128, NT])
    stt = sbt("stt", [128, 16])
    spc = sbt("spc", [128, 4])
    _spk = [0]
    nt_a = sbt("nt_a", [128, NT])
    scr = {}
    psA = es.enter_context(nc.psum_tensor("psA", [128, 512], F32))
    psB = es.enter_context(nc.psum_tensor("psB", [128, 512], F32))
    psE = es.enter_context(nc.psum_tensor("psE", [128, 512], F32))
    psC = es.enter_context(nc.psum_tensor("psC", [128, 1024], F32))
    psD = es.enter_context(nc.psum_tensor("psD", [128, 1024], F32))

    S = Sched(nc, es)
    D_S, D_G, TM, VM, SM, ST = "dS", "dG", "tm", "vm", "sm", "st"

    S.dma("sync", x_sb[:].rearrange("p (i d) -> p i d", d=D),
          x_d[0].rearrange("(i p) d -> p i d", p=128), D_S)
    for name, tile in (("ident", ident), ("e2t", e2t_sb), ("biasv", biasv_sb),
                       ("tri", tri_sb), ("ones1", ones1_sb), ("ones128", ones128_sb), ("epsv", eps_sb),
                       ("onesr", onesr_sb), ("ones128b", ones128b_sb)):
        S.dma("sync", tile[:], dram[name][:, :], D_S)
    S.dma("gpsimd", e2_sb[:], dram["e2"][:, :], D_G)
    S.dma("gpsimd", ai_sb[:, 0:1], dram["asc1"][None, :].broadcast_to((128, 1)), D_G)
    S.dma("gpsimd", ai_sb[:, 1:2], dram["abi1"][None, :].broadcast_to((128, 1)), D_G)
    S.emit("vector", lambda v: v.memset(spc[:], 0.0), "vm")
    S.emit("vector", lambda v: v.memset(nt_a[:], 0.0), "vm")

    def V(fn, inc=VM):
        S.emit("vector", fn, inc)

    def _spacer(inc):
        k = _spk[0] % 4
        _spk[0] += 1
        S.emit("vector", lambda v, k=k: v.tensor_copy(spc[:, k:k + 1],
                                                      eps_sb[:, 0:1]), inc)

    def Vt(fn):
        S.emit("vector", fn, None)
        _spacer(VM)

    def Vred(fn):
        S.emit("vector", fn, VM)
        _spacer(None)
        _spacer(VM)
        S.wait("vector", VM)

    def SC(fn, inc=SM):
        S.emit("scalar", fn, inc)

    def TE(fn, inc=None):
        S.emit("tensor", fn, inc)

    def xt(i):
        return x_sb[:, i * D:(i + 1) * D]

    def ternarize(tes, wt_d, R, C, out_sb, ws_col, stash_d):
        nsl = R // 128
        sc = lambda n, sh, dt=F32: tes.enter_context(nc.sbuf_tensor(n, list(sh), dt))
        stg = [sc(f"tg{ws_col}a", [128, C]), sc(f"tg{ws_col}b", [128, C])]
        pacc = sc(f"pc{ws_col}", [128, nsl])
        ab = sc(f"ta{ws_col}", [128, 2048])
        chans = ("twA", "twB")
        for s in range(nsl):
            buf = stg[s % 2]
            ch = chans[s % 2]
            S.wait("gpsimd", VM)
            S.dma("gpsimd", buf[:], wt_d[s * 128:(s + 1) * 128, :], ch)
            S.wait("vector", ch)
            Vred(lambda v, b=buf, s=s: v.tensor_reduce(
                pacc[:, s:s + 1], b[:], AX, ALU.add, apply_absolute_value=True))
        Vred(lambda v, nsl=nsl: v.tensor_reduce(stt[:, 0:1], pacc[:, 0:nsl], AX, ALU.add))
        _spacer(VM)
        S.wait("tensor", VM)
        TE(lambda t: t.matmul(psE[0:1, 0:1], ones128_sb[:], stt[:, 0:1],
                              start=True, stop=True), TM)
        S.wait("vector", TM)
        Vt(lambda v: v.tensor_copy(stt[0:1, 1:2], psE[0:1, 0:1]))
        _spacer(VM)
        S.wait("tensor", VM)
        TE(lambda t: t.matmul(psE[:, 4:5], ones1_sb[:], stt[0:1, 1:2],
                              start=True, stop=True), TM)
        S.wait("vector", TM)
        mmul = 1.0 / float(R * C)
        Vt(lambda v: v.tensor_scalar(stt[:, 2:3], psE[:, 4:5], mmul, None, ALU.mult))
        Vt(lambda v, w=ws_col: v.tensor_scalar(ws_sb[:, w:w + 1], stt[:, 2:3],
                                               1e-5, None, ALU.max))
        Vt(lambda v, w=ws_col: v.tensor_scalar(stt[:, 3:4], ws_sb[:, w:w + 1],
                                               0.5, None, ALU.mult))
        Vt(lambda v, w=ws_col: v.tensor_scalar(stt[:, 4:5], ws_sb[:, w:w + 1],
                                               -0.5, None, ALU.mult))
        for s in range(nsl):
            buf = stg[s % 2]
            ch = chans[s % 2]
            S.wait("gpsimd", VM)
            S.dma("gpsimd", buf[:], wt_d[s * 128:(s + 1) * 128, :], ch)
            S.wait("vector", ch)
            for j in range(0, C, 1024):
                w = min(1024, C - j)
                V(lambda v, b=buf, j=j, w=w: v.tensor_scalar(
                    ab[:, 0:w], b[:, j:j + w], stt[:, 3:4], None, ALU.is_ge), None)
                V(lambda v, b=buf, j=j, w=w: v.tensor_scalar(
                    ab[:, 1024:1024 + w], b[:, j:j + w], stt[:, 4:5], None,
                    ALU.is_le), None)
                V(lambda v, s=s, j=j, w=w: v.tensor_sub(
                    out_sb[:, s * C + j:s * C + j + w], ab[:, 0:w],
                    ab[:, 1024:1024 + w]))
        S.wait("gpsimd", VM)
        S.dma("gpsimd", stash_d.rearrange("(s p) c -> p s c", p=128),
              out_sb[:, 0:nsl * C].rearrange("p (s c) -> p s c", c=C), "twS")

    def load_fp8(stash_d, out_sb, nsl, C):
        S.wait("gpsimd", "twS")
        S.dma("gpsimd", out_sb[:, 0:nsl * C].rearrange("p (s c) -> p s c", c=C),
              stash_d.rearrange("(s p) c -> p s c", p=128), D_G)

    def ln_prepass(want_r2, junk):
        for i in range(NT):
            Vred(lambda v, i=i: v.reduce_sum(stt[:, 0:1], xt(i), AX))
            V(lambda v, i=i: v.tensor_scalar(st_m[:, i:i + 1], stt[:, 0:1],
                                             1.0 / D, None, ALU.mult))
            S.wait("scalar", VM)
            SC(lambda s, i=i, junk=junk: s.activation(
                junk[:, 0:D], xt(i), AF.Square, accum_out=stt[:, 1:2]))
            S.wait("vector", SM)
            Vt(lambda v: v.tensor_scalar(stt[:, 2:3], stt[:, 1:2], 1.0 / D,
                                         None, ALU.mult))
            Vt(lambda v, i=i: v.tensor_mul(stt[:, 3:4], st_m[:, i:i + 1],
                                           st_m[:, i:i + 1]))
            Vt(lambda v, i=i: v.tensor_sub(st_v[:, i:i + 1], stt[:, 2:3],
                                           stt[:, 3:4]))
        Vt(lambda v: v.tensor_scalar(st_v[:], st_v[:], 1e-5, None, ALU.add))
        S.wait("scalar", VM)
        SC(lambda s: s.activation(st_r[:], st_v[:], AF.Ln, bias=0.0, scale=1.0))
        SC(lambda s: s.activation(st_r[:], st_r[:], AF.Exp, bias=0.0, scale=-0.5))
        S.wait("vector", SM)
        newton_rsqrt(st_r[:], st_v[:], NT)
        if want_r2:
            Vt(lambda v: v.tensor_mul(st_r2[:], st_r[:], st_r[:]))
            Vt(lambda v: v.tensor_scalar(st_v[:], st_v[:], -1e-5, None, ALU.add))
            Vt(lambda v: v.tensor_mul(st_r2[:], st_r2[:], st_v[:]))
            Vt(lambda v: v.tensor_scalar(st_r2[:], st_r2[:], 1e-8, None, ALU.add))
            Vt(lambda v: v.tensor_copy(st_v[:], st_r2[:]))
            S.wait("scalar", VM)
            SC(lambda s: s.activation(st_r2[:], st_r2[:], AF.Ln, bias=0.0, scale=1.0))
            SC(lambda s: s.activation(st_r2[:], st_r2[:], AF.Exp, bias=0.0, scale=-0.5))
            S.wait("vector", SM)
            newton_rsqrt(st_r2[:], st_v[:], NT)
        S.wait("vector", SM)

    def newton_rsqrt(rt, vt2, W):
        Vt(lambda v: v.tensor_mul(nt_a[:, 0:W], rt, rt))
        Vt(lambda v: v.tensor_mul(nt_a[:, 0:W], nt_a[:, 0:W], vt2))
        Vt(lambda v: v.tensor_scalar(nt_a[:, 0:W], nt_a[:, 0:W], -0.5, 1.5,
                                     ALU.mult, ALU.add))
        Vt(lambda v: v.tensor_mul(rt, rt, nt_a[:, 0:W]))

    def quant(src_ap, s_ap, dst_ap, W):
        qi32, dbuf, ebuf = scr["qi32"], scr["dbuf"], scr["ebuf"]
        S.wait("scalar", VM)
        SC(lambda s: s.activation(qi32[:, 0:W], src_ap, AF.Copy, bias=-1.0,
                                  scale=s_ap))
        S.wait("vector", SM)
        V(lambda v: v.scalar_tensor_tensor(dbuf[:, 0:W], src_ap, s_ap,
                                           qi32[:, 0:W], ALU.mult, ALU.subtract))
        V(lambda v: v.scalar_tensor_tensor(ebuf[:, 0:W], dbuf[:, 0:W], 1.5,
                                           qi32[:, 0:W], ALU.is_gt, ALU.add))
        V(lambda v: v.scalar_tensor_tensor(dst_ap, dbuf[:, 0:W], 0.5,
                                           ebuf[:, 0:W], ALU.is_ge, ALU.add))

    def ffn_phase(l, seq, half, pes):
        sc = lambda n, sh, dt=F32: pes.enter_context(nc.sbuf_tensor(n, list(sh), dt))
        tag = f"f{l}{seq}{half}"
        w1q = sc(f"w1q{tag}", [128, DC * HF], FP8)
        w2q = sc(f"w2q{tag}", [128, HC * D], FP8)

        if seq == 0 and half == 0:
            with ExitStack() as tes:
                ternarize(tes, dram[f"w1t{l}"], D, HF, w1q, 2 * l, wscr[(l, 1)])
                ternarize(tes, dram[f"w2t{l}"], HF, D, w2q, 2 * l + 1, wscr[(l, 2)])
        else:
            load_fp8(wscr[(l, 1)], w1q, DC, HF)
            load_fp8(wscr[(l, 2)], w2q, HC, D)

        psT = pes.enter_context(nc.psum_tensor(f"psT{tag}", [128, 1024], BF16))
        tbuf = scr["tbuf"] = sc(f"tbuf{tag}", [128, D])
        xq_tm = scr["xq_tm"] = sc(f"xq{tag}", [128, D], BF16)
        xqT = scr["xqT"] = sc(f"xqT{tag}", [128, D], BF16)
        scr["qi32"] = sc(f"qi32{tag}", [128, 1024], I32)
        scr["dbuf"] = sc(f"dbuf{tag}", [128, 1024])
        scr["ebuf"] = sc(f"ebuf{tag}", [128, 1024])
        h_sb = scr["h_sb"] = sc(f"h{tag}", [128, HF])
        hq_tm = scr["hq_tm"] = sc(f"hq{tag}", [128, HF], BF16)
        hqT = scr["hqT"] = sc(f"hqT{tag}", [128, HF], BF16)
        if l == 0:
            b1bc = sc(f"b1bc{tag}", [128, HF])
            b2bc = sc(f"b2bc{tag}", [128, D])
            b2h_bc = sc(f"b2h{tag}", [128, D])
            S.wait("sync", VM)
            S.dma("sync", b1bc[:], dram["b1_0"][None, :].broadcast_to((128, HF)), D_S)
            S.dma("sync", b2bc[:], dram["b2_0"][None, :].broadcast_to((128, D)), D_S)
        else:
            n1bc = sc(f"n1bc{tag}", [128, D])
            n2bc = sc(f"n2bc{tag}", [128, HF])
            S.wait("sync", VM)
            S.dma("sync", n1bc[:], dram["n1w1"][None, :].broadcast_to((128, D)), D_S)
            S.dma("sync", n2bc[:], dram["n2w1"][None, :].broadcast_to((128, HF)), D_S)

        ln_prepass(want_r2=(l == 1), junk=xq_tm)
        S.wait("vector", D_S)
        S.wait("scalar", D_S)
        S.wait("tensor", D_G)
        S.wait("vector", D_G)
        if l == 0:
            V(lambda v: v.tensor_scalar(b2h_bc[:], b2bc[:], 0.5, None, ALU.mult))

        for i in range(NT):
            m_ap = st_m[:, i:i + 1]
            r_ap = st_r[:, i:i + 1]
            if l == 0:
                V(lambda v, i=i, m=m_ap: v.tensor_scalar(
                    tbuf[:, 0:D], xt(i), m, None, ALU.subtract))
            else:
                V(lambda v, i=i, m=m_ap: v.scalar_tensor_tensor(
                    tbuf[:, 0:D], xt(i), m, n1bc[:], ALU.subtract, ALU.mult))
            Vred(lambda v: v.tensor_reduce(stt[:, 5:6], tbuf[:, 0:D], AX,
                                             ALU.max, apply_absolute_value=True))
            if l == 0:
                Vt(lambda v, r=r_ap: v.tensor_copy(stt[:, 9:10], r))
            else:
                Vt(lambda v, i=i: v.tensor_mul(stt[:, 9:10], st_r[:, i:i + 1],
                                               st_r2[:, i:i + 1]))
            Vt(lambda v: v.tensor_mul(stt[:, 6:7], stt[:, 5:6], stt[:, 9:10]))
            Vt(lambda v: v.tensor_scalar(stt[:, 6:7], stt[:, 6:7], 1e-5, None, ALU.max))
            Vt(lambda v: v.reciprocal(stt[:, 7:8], stt[:, 6:7]))
            Vt(lambda v: v.tensor_scalar(stt[:, 7:8], stt[:, 7:8], 127.0, None,
                                         ALU.mult))
            Vt(lambda v: v.tensor_mul(stt[:, 8:9], stt[:, 7:8], stt[:, 9:10]))
            Vt(lambda v: v.tensor_scalar(stt[:, 10:11], stt[:, 6:7], 1.0 / 127.0,
                                         None, ALU.mult))
            Vt(lambda v, l=l: v.tensor_mul(stt[:, 10:11], stt[:, 10:11],
                                           ws_sb[:, 2 * l:2 * l + 1]))
            quant(tbuf[:, 0:D], stt[:, 8:9], xq_tm[:, 0:D], D)
            S.wait("tensor", VM)
            for k in range(DC):
                TE(lambda t, k=k: t.transpose(psT[:, k * 128:(k + 1) * 128],
                                              xq_tm[:, k * 128:(k + 1) * 128],
                                              ident[:]), TM if k == DC - 1 else None)
            S.wait("vector", TM)
            V(lambda v: v.tensor_copy(xqT[:, 0:D], psT[:, 0:D]))
            for n in range(8):
                pb = psA if n % 2 == 0 else psB
                S.wait("tensor", VM)
                for k in range(DC):
                    TE(lambda t, k=k, n=n, pb=pb: t.matmul(
                        pb[:, 0:512], xqT[:, k * 128:(k + 1) * 128],
                        w1q[:, k * HF + n * 512:k * HF + n * 512 + 512],
                        start=(k == 0), stop=(k == DC - 1)),
                       TM if k == DC - 1 else None)
                if l == 0:
                    S.wait("vector", TM)
                    V(lambda v, n=n, pb=pb: v.scalar_tensor_tensor(
                        h_sb[:, n * 512:(n + 1) * 512], pb[:, 0:512],
                        stt[:, 10:11], b1bc[:, n * 512:(n + 1) * 512],
                        ALU.mult, ALU.add))
                    S.wait("scalar", VM)
                    SC(lambda s, n=n: s.activation(
                        h_sb[:, n * 512:(n + 1) * 512],
                        h_sb[:, n * 512:(n + 1) * 512], AF.Gelu))
                else:
                    S.wait("scalar", TM)
                    SC(lambda s, n=n, pb=pb: s.activation(
                        h_sb[:, n * 512:(n + 1) * 512], pb[:, 0:512], AF.Relu,
                        bias=0.0, scale=stt[:, 10:11]))
                    SC(lambda s, n=n: s.activation(
                        h_sb[:, n * 512:(n + 1) * 512],
                        h_sb[:, n * 512:(n + 1) * 512], AF.Square))
                    S.wait("vector", SM)
                    V(lambda v, n=n: v.tensor_scalar(
                        h_sb[:, n * 512:(n + 1) * 512],
                        h_sb[:, n * 512:(n + 1) * 512],
                        ai_sb[:, 0:1], ai_sb[:, 1:2], ALU.mult, ALU.add))
            S.wait("vector", SM)
            if l == 1:
                S.wait("scalar", VM)
                SC(lambda s: s.activation(hq_tm[:, 0:HF], h_sb[:, 0:HF],
                                          AF.Square, accum_out=stt[:, 11:12]))
                S.wait("vector", SM)
                Vt(lambda v: v.tensor_scalar(stt[:, 11:12], stt[:, 11:12],
                                             1.0 / HF, 1e-8, ALU.mult, ALU.add))
                Vt(lambda v: v.tensor_copy(stt[:, 15:16], stt[:, 11:12]))
                S.wait("scalar", VM)
                SC(lambda s: s.activation(stt[:, 11:12], stt[:, 11:12], AF.Ln,
                                          bias=0.0, scale=1.0))
                SC(lambda s: s.activation(stt[:, 11:12], stt[:, 11:12], AF.Exp,
                                          bias=0.0, scale=-0.5))
                S.wait("vector", SM)
                newton_rsqrt(stt[:, 11:12], stt[:, 15:16], 1)
                V(lambda v: v.scalar_tensor_tensor(
                    h_sb[:, 0:HF], h_sb[:, 0:HF], stt[:, 11:12], n2bc[:],
                    ALU.mult, ALU.mult))
            Vred(lambda v: v.tensor_reduce(stt[:, 12:13], h_sb[:, 0:HF], AX,
                                             ALU.max, apply_absolute_value=True))
            Vt(lambda v: v.tensor_scalar(stt[:, 12:13], stt[:, 12:13], 1e-5,
                                         None, ALU.max))
            Vt(lambda v: v.reciprocal(stt[:, 13:14], stt[:, 12:13]))
            Vt(lambda v: v.tensor_scalar(stt[:, 13:14], stt[:, 13:14], 127.0,
                                         None, ALU.mult))
            Vt(lambda v: v.tensor_scalar(stt[:, 14:15], stt[:, 12:13],
                                         0.5 / 127.0, None, ALU.mult))
            Vt(lambda v, l=l: v.tensor_mul(stt[:, 14:15], stt[:, 14:15],
                                           ws_sb[:, 2 * l + 1:2 * l + 2]))
            for j in range(0, HF, 1024):
                quant(h_sb[:, j:j + 1024], stt[:, 13:14], hq_tm[:, j:j + 1024], 1024)
            for rr in range(4):
                S.wait("tensor", VM)
                for k in range(8):
                    kk = rr * 8 + k
                    TE(lambda t, kk=kk, k=k: t.transpose(
                        psT[:, k * 128:(k + 1) * 128],
                        hq_tm[:, kk * 128:(kk + 1) * 128], ident[:]),
                       TM if k == 7 else None)
                S.wait("vector", TM)
                V(lambda v, rr=rr: v.tensor_copy(
                    hqT[:, rr * 1024:(rr + 1) * 1024], psT[:, 0:1024]))
            S.wait("tensor", VM)
            for n in range(2):
                for k in range(HC):
                    TE(lambda t, k=k, n=n: t.matmul(
                        psC[:, n * 512:(n + 1) * 512],
                        hqT[:, k * 128:(k + 1) * 128],
                        w2q[:, k * D + n * 512:k * D + n * 512 + 512],
                        start=(k == 0), stop=(k == HC - 1)),
                       TM if (k == HC - 1 and n == 1) else None)
            S.wait("vector", TM)
            if l == 0:
                V(lambda v, i=i: v.tensor_add(xt(i), xt(i), b2h_bc[:]))
            V(lambda v, i=i: v.scalar_tensor_tensor(
                xt(i), psC[:, 0:1024], stt[:, 14:15], xt(i), ALU.mult, ALU.add))
            if DEBUG_TAPS and seq == 0 and half == 0 and l == 0 and i in (0, 8):
                ti = 0 if i == 0 else 1
                S.wait("sync", VM)
                S.wait("sync", SM)
                S.dma("sync", taps["t_xq"][ti], xq_tm[:, 0:D], ST)
                S.dma("sync", taps["t_h"][ti], h_sb[:, 0:HF], ST)
                S.dma("sync", taps["t_hq"][ti], hq_tm[:, 0:HF], ST)
                S.dma("sync", taps["t_stt"][ti], stt[:, 0:16], ST)
                S.dma("sync", taps["t_m"][ti], st_m[:, i:i+1], ST)
                S.dma("sync", taps["t_r"][ti], st_r[:, i:i+1], ST)
                S.dma("sync", taps["t_vv"][ti], st_v[:, i:i+1], ST)
                S.wait("sync", ST)
            if DEBUG_TAPS and seq == 0 and half == 0 and l == 0 and i == 0:
                S.dma("sync", taps["t_w1q"][:, :], w1q[:, :], ST)
                S.dma("sync", taps["t_w2q"][:, :], w2q[:, :], ST)
                S.dma("sync", taps["t_ws"][:, :], ws_sb[:, :], ST)
                if l == 0:
                    S.dma("sync", taps["t_b1"][:, :], b1bc[:, :], ST)
                S.wait("sync", ST)

    def attn_phase(l, seq, pes):
        sc = lambda n, sh, dt=F32: pes.enter_context(nc.sbuf_tensor(n, list(sh), dt))
        tag = f"a{l}{seq}"
        xhT = sc(f"xhT{tag}", [128, DC * T], BF16)
        vprime = sc(f"vp{tag}", [128, NT * NH * 65], BF16)
        wbig = sc(f"wbig{tag}", [128, DC * D], BF16)
        ofm = sc(f"ofm{tag}", [128, DC * 1024], BF16)
        qc_sb = sc(f"qc{tag}", [128, T], BF16)
        kc_sb = sc(f"kc{tag}", [128, T], BF16)
        wqk = sc(f"wqk{tag}", [128, DC * 256], BF16)
        pb0 = sc(f"pb0{tag}", [128, 1024], BF16)
        pb1 = sc(f"pb1{tag}", [128, 1024], BF16)
        rs_sb = sc(f"rs{tag}", [65, 1024])
        rcp = sc(f"rcp{tag}", [128, 1024])
        batt = sc(f"batt{tag}", [128, 4 * DC])
        bvbc = sc(f"bvbc{tag}", [128, D])
        bobc = sc(f"bobc{tag}", [128, D])
        qraw = sc(f"qraw{tag}", [128, 512])
        kraw = sc(f"kraw{tag}", [128, 512])
        sqb = sc(f"sqb{tag}", [128, 512], BF16)
        l2v = sc(f"l2v{tag}", [2, 512])
        xh_tm = sc(f"xh{tag}", [128, D], BF16)

        for w, b in ((0, "bq"), (1, "bk"), (2, "bv"), (3, "bo")):
            S.dma("sync", batt[:, w * DC:(w + 1) * DC],
                  dram[f"{b}{l}"].rearrange("(c p) -> p c", p=128), D_S, slow=True)
        S.dma("sync", bvbc[:], dram[f"bv{l}"][None, :].broadcast_to((128, D)), D_S)
        S.dma("sync", bobc[:], dram[f"bo{l}"][None, :].broadcast_to((128, D)), D_S)
        S.dma("gpsimd", wbig[:].rearrange("p (k o) -> p k o", o=D),
              dram[f"wvt{l}"].rearrange("(k p) o -> p k o", p=128), D_G)

        ln_prepass(want_r2=False, junk=xh_tm)
        S.wait("vector", D_S)
        if l == 1:
            V(lambda v: v.tensor_scalar(bobc[:], bobc[:], 0.5, None, ALU.mult))

        with ExitStack() as tscope:
            psT = tscope.enter_context(nc.psum_tensor(f"psT{tag}", [128, 1024], BF16))
            for i in range(NT):
                V(lambda v, i=i: v.tensor_scalar(
                    xh_tm[:], xt(i), st_m[:, i:i + 1], st_r[:, i:i + 1],
                    ALU.subtract, ALU.mult))
                S.wait("tensor", VM)
                for k in range(DC):
                    TE(lambda t, k=k, psT=psT: t.transpose(
                        psT[:, k * 128:(k + 1) * 128],
                        xh_tm[:, k * 128:(k + 1) * 128],
                        ident[:]), TM if k == DC - 1 else None)
                S.wait("vector", TM)
                V(lambda v, i=i, psT=psT: v.tensor_copy(
                    xhT[:].rearrange("p (k t) -> p k t", t=T)[:, :, i * 128:(i + 1) * 128],
                    psT[:, 0:D].rearrange("p (k c) -> p k c", c=128)))
        psR = pes.enter_context(nc.psum_tensor(f"psR{tag}", [128, 512], F32))

        S.wait("tensor", D_G)
        S.wait("tensor", VM)
        for i in range(NT):
            S.wait("tensor", VM)
            for n in range(2):
                pbn = psC if n == 0 else psD
                for k in range(DC):
                    TE(lambda t, i=i, n=n, k=k, pbn=pbn: t.matmul(
                        pbn[:, 0:512],
                        xhT[:, k * T + i * 128:k * T + (i + 1) * 128],
                        wbig[:, k * D + n * 512:k * D + n * 512 + 512],
                        start=(k == 0), stop=(k == DC - 1)),
                       TM if k == DC - 1 else None)
            S.wait("vector", TM)
            for n in range(2):
                pbn = psC if n == 0 else psD
                V(lambda v, i=i, n=n, pbn=pbn: v.tensor_tensor(
                    vprime[:, i * NH * 65:(i + 1) * NH * 65]
                    .rearrange("p (h c) -> p h c", c=65)[:, 8 * n:8 * n + 8, 0:64],
                    pbn[:, 0:512].rearrange("p (h c) -> p h c", c=64),
                    bvbc[:, n * 512:(n + 1) * 512]
                    .rearrange("p (h c) -> p h c", c=64), ALU.add))
            V(lambda v, i=i: v.memset(
                vprime[:, i * NH * 65:(i + 1) * NH * 65]
                .rearrange("p (h c) -> p h c", c=65)[:, :, 64:65], 1.0))
        S.wait("gpsimd", VM)
        S.dma("gpsimd", wbig[:].rearrange("p (k o) -> p k o", o=D),
              dram[f"wot{l}"].rearrange("(k p) o -> p k o", p=128), D_G)

        for H in range(2):
            qlo_t, qhi_t = H * 8, H * 8 + 8
            for c in range(8):
                S.wait("gpsimd", TM)
                S.wait("gpsimd", VM)
                S.dma("gpsimd", wqk[:, 0:DC * 128].rearrange("p (k o) -> p k o", o=128),
                      dram[f"wqt{l}"][:, c * 128:(c + 1) * 128]
                      .rearrange("(k p) o -> p k o", p=128), D_G)
                S.dma("gpsimd", wqk[:, DC * 128:].rearrange("p (k o) -> p k o", o=128),
                      dram[f"wkt{l}"][:, c * 128:(c + 1) * 128]
                      .rearrange("(k p) o -> p k o", p=128), D_G)
                S.wait("tensor", D_G)
                for qk in range(2):
                    dst = qc_sb if qk == 0 else kc_sb
                    raw = qraw if qk == 0 else kraw
                    wofs = qk * DC * 128
                    for s4 in range(4):
                        pbk = psA if s4 % 2 == 0 else psB
                        S.wait("tensor", VM)
                        S.wait("tensor", SM)
                        for k in range(DC):
                            TE(lambda t, k=k, s4=s4, pbk=pbk, wofs=wofs: t.matmul(
                                pbk[:, 0:512],
                                wqk[:, wofs + k * 128:wofs + (k + 1) * 128],
                                xhT[:, k * T + s4 * 512:k * T + (s4 + 1) * 512],
                                start=(k == 0), stop=(k == DC - 1)),
                               TM if k == DC - 1 else None)
                        if l == 0:
                            S.wait("vector", TM)
                            if qk == 0:
                                V(lambda v, s4=s4, pbk=pbk, dst=dst, c=c: v.tensor_scalar(
                                    dst[:, s4 * 512:(s4 + 1) * 512], pbk[:, 0:512],
                                    batt[:, c:c + 1], 0.125, ALU.add, ALU.mult))
                            else:
                                V(lambda v, s4=s4, pbk=pbk, dst=dst, c=c: v.tensor_scalar(
                                    dst[:, s4 * 512:(s4 + 1) * 512], pbk[:, 0:512],
                                    batt[:, DC + c:DC + c + 1], None, ALU.add))
                        else:
                            bcol = (0 if qk == 0 else DC) + c
                            S.wait("vector", TM)
                            V(lambda v, pbk=pbk, raw=raw, bcol=bcol: v.tensor_scalar(
                                raw[:], pbk[:, 0:512], batt[:, bcol:bcol + 1],
                                None, ALU.add))
                            V(lambda v, raw=raw: v.tensor_mul(sqb[:], raw[:], raw[:]))
                            S.wait("tensor", VM)
                            TE(lambda t: t.matmul(psE[0:2, 0:512], e2_sb[:], sqb[:],
                                                  start=True, stop=True), TM)
                            S.wait("vector", TM)
                            V(lambda v: v.tensor_scalar(l2v[:], psE[0:2, 0:512],
                                                        1e-30, None, ALU.max))
                            S.wait("scalar", VM)
                            SC(lambda s: s.activation(l2v[:], l2v[:], AF.Ln,
                                                      bias=0.0, scale=1.0))
                            SC(lambda s: s.activation(l2v[:], l2v[:], AF.Exp,
                                                      bias=0.0, scale=0.5))
                            S.wait("vector", SM)
                            V(lambda v: v.tensor_scalar(l2v[:], l2v[:], 1e-8,
                                                        None, ALU.max))
                            V(lambda v: v.reciprocal(l2v[:], l2v[:]))
                            if qk == 0:
                                V(lambda v: v.tensor_scalar(
                                    l2v[:], l2v[:], float(1.0 / np.sqrt(128.0)),
                                    None, ALU.mult))
                            S.wait("tensor", VM)
                            TE(lambda t: t.matmul(psE[:, 0:512], e2t_sb[:], l2v[:],
                                                  start=True, stop=True), TM)
                            S.wait("vector", TM)
                            V(lambda v, raw=raw, dst=dst, s4=s4: v.tensor_mul(
                                dst[:, s4 * 512:(s4 + 1) * 512], raw[:],
                                psE[:, 0:512]))
                kc_lo = max(0, qlo_t - (L0_WIN - 1)) if l == 0 else 0

                def spans_for(kc):
                    ql = max(kc, qlo_t) * 128
                    qh = (min(kc + L0_WIN, qhi_t) if l == 0 else qhi_t) * 128
                    return [(sp, min(512, qh - sp)) for sp in range(ql, qh, 512)]

                def bank_subspans(sp, spw):
                    res, a = [], sp
                    base = qlo_t * 128
                    while a < sp + spw:
                        b = min(sp + spw, base + ((a - base) // 512 + 1) * 512)
                        res.append((a, b))
                        a = b
                    return res

                order = []
                for kc in range(kc_lo, qhi_t):
                    for (sp, spw) in spans_for(kc):
                        for which in ("E", "O", "S"):
                            for (a, b) in bank_subspans(sp, spw):
                                bank = (a - qlo_t * 128) // 512
                                key = (which if which != "E" else "C", bank)
                                order.append(key)
                first_ix, last_ix = {}, {}
                for ix, key in enumerate(order):
                    if key not in first_ix:
                        first_ix[key] = ix
                    last_ix[key] = ix
                oix = 0
                for kc in range(kc_lo, qhi_t):
                    for (sp, spw) in spans_for(kc):
                        pbk = psA
                        S.wait("tensor", SM)
                        S.wait("tensor", VM)
                        for p01 in (0, 1):
                            TE(lambda t, kc=kc, p01=p01, sp=sp, spw=spw, pbk=pbk:
                               t.matmul((pbk if p01 == 0 else psB)[:, 0:spw],
                                        kc_sb[p01 * 64:(p01 + 1) * 64,
                                              kc * 128:(kc + 1) * 128],
                                        qc_sb[p01 * 64:(p01 + 1) * 64, sp:sp + spw],
                                        start=True, stop=True,
                                        tile_position=(p01 * 64, 0)),
                               TM if p01 == 1 else None)
                        S.wait("scalar", TM)
                        S.wait("scalar", VM)
                        for qi in range(sp // 128, (sp + spw) // 128):
                            off = qi * 128 - sp
                            qof = qi * 128 - qlo_t * 128
                            for p01 in (0, 1):
                                src_ps = pbk if p01 == 0 else psB
                                dst = pb0 if p01 == 0 else pb1
                                if l == 0:
                                    dlt = qi - kc
                                    SC(lambda s, src_ps=src_ps, dst=dst, off=off,
                                             qof=qof, dlt=dlt:
                                       s.activation(dst[:, qof:qof + 128],
                                                    src_ps[:, off:off + 128], AF.Exp,
                                                    bias=biasv_sb[:, dlt:dlt + 1],
                                                    scale=1.0))
                                else:
                                    SC(lambda s, src_ps=src_ps, dst=dst, off=off, qof=qof:
                                       s.activation(dst[:, qof:qof + 128],
                                                    src_ps[:, off:off + 128], AF.Exp,
                                                    bias=0.0, scale=1.0 / 1.5))
                            if qi == kc:
                                S.wait("vector", SM)
                                V(lambda v, qof=qof: v.tensor_mul(
                                    pb0[:, qof:qof + 128], pb0[:, qof:qof + 128],
                                    tri_sb[:]))
                                V(lambda v, qof=qof: v.tensor_mul(
                                    pb1[:, qof:qof + 128], pb1[:, qof:qof + 128],
                                    tri_sb[:]))
                        if DEBUG_STOP:
                            raise _StopEmit()
                        S.wait("tensor", SM)
                        S.wait("tensor", VM)
                        subs_ab = bank_subspans(sp, spw)
                        newbanks = []
                        _o2 = oix
                        for wi in range(3):
                            for (a, b) in subs_ab:
                                if _o2 == first_ix[order[_o2]]:
                                    bnk = (a - qlo_t * 128) // 512
                                    if (bnk, a - qlo_t * 128, b - qlo_t * 128) not in newbanks:
                                        newbanks.append((bnk, a - qlo_t * 128,
                                                         b - qlo_t * 128))
                                _o2 += 1
                        for (bnk, v0, v1) in newbanks:
                            if v0 > bnk * 512:
                                V(lambda v, bnk=bnk, v0=v0: v.memset(
                                    pb0[:, bnk * 512:v0], 0.0), None)
                                V(lambda v, bnk=bnk, v0=v0: v.memset(
                                    pb1[:, bnk * 512:v0], 0.0), None)
                            if v1 < bnk * 512 + 512:
                                V(lambda v, bnk=bnk, v1=v1: v.memset(
                                    pb0[:, v1:bnk * 512 + 512], 0.0), None)
                                V(lambda v, bnk=bnk, v1=v1: v.memset(
                                    pb1[:, v1:bnk * 512 + 512], 0.0), VM)
                        if newbanks:
                            S.wait("tensor", VM)
                        for wi, which in enumerate(("E", "O", "S")):
                            for ai, (a, b) in enumerate(subs_ab):
                                stf = (oix == first_ix[order[oix]])
                                spf = (oix == last_ix[order[oix]])
                                c0 = a - qlo_t * 128
                                c1 = b - qlo_t * 128
                                if stf:
                                    bnk = c0 // 512
                                    c0 = bnk * 512
                                    c1 = bnk * 512 + 512
                                lastmm = (wi == 2 and ai == len(subs_ab) - 1)
                                if which == "E":
                                    TE(lambda t, kc=kc, c0=c0, c1=c1,
                                             stf=stf, spf=spf, c=c:
                                       t.matmul(psC[0:65, c0:c1],
                                                vprime[:, kc * NH * 65 + (2 * c) * 65:
                                                       kc * NH * 65 + (2 * c) * 65 + 65],
                                                pb0[:, c0:c1], start=stf, stop=spf),
                                       None)
                                elif which == "O":
                                    TE(lambda t, kc=kc, c0=c0, c1=c1,
                                             stf=stf, spf=spf, c=c:
                                       t.matmul(psD[64:128, c0:c1],
                                                vprime[:, kc * NH * 65 + (2 * c + 1) * 65:
                                                       kc * NH * 65 + (2 * c + 1) * 65 + 64],
                                                pb1[:, c0:c1], start=stf, stop=spf,
                                                tile_position=(0, 64)),
                                       None)
                                else:
                                    bnk = c0 // 512
                                    TE(lambda t, c0=c0, c1=c1, bnk=bnk,
                                             stf=stf, spf=spf, psR=psR:
                                       t.matmul((psE if bnk == 0 else psR)
                                                [0:1, c0 - 512 * bnk:c1 - 512 * bnk],
                                                ones128b_sb[:, 0:1],
                                                pb1[:, c0:c1], start=stf, stop=spf),
                                       TM if lastmm else None)
                                oix += 1
                S.wait("vector", TM)
                V(lambda v: v.tensor_copy(rs_sb[64:65, 0:1024], psC[64:65, 0:1024]))
                V(lambda v: v.tensor_copy(rs_sb[0:1, 0:512], psE[0:1, 0:512]))
                V(lambda v, psR=psR: v.tensor_copy(rs_sb[0:1, 512:1024],
                                                   psR[0:1, 0:512]))
                for sp in range(2):
                    S.wait("tensor", VM)
                    TE(lambda t, sp=sp: t.matmul(
                        psA[:, 0:512], onesr_sb[64:65, 0:128],
                        rs_sb[64:65, sp * 512:(sp + 1) * 512],
                        start=True, stop=True, tile_position=(64, 0)), TM)
                    TE(lambda t, sp=sp: t.matmul(
                        psB[:, 0:512], ones1_sb[0:1, 0:128],
                        rs_sb[0:1, sp * 512:(sp + 1) * 512],
                        start=True, stop=True), TM)
                    S.wait("vector", TM)
                    V(lambda v: v.reciprocal(rcp[:, 0:512], psA[:, 0:512]))
                    V(lambda v: v.reciprocal(rcp[:, 512:1024], psB[:, 0:512]))
                    V(lambda v, sp=sp, c=c: v.tensor_mul(
                        ofm[0:64, c * 1024 + sp * 512:c * 1024 + (sp + 1) * 512],
                        psC[0:64, sp * 512:(sp + 1) * 512], rcp[0:64, 0:512]))
                    V(lambda v, sp=sp, c=c: v.tensor_mul(
                        ofm[64:128, c * 1024 + sp * 512:c * 1024 + (sp + 1) * 512],
                        psD[64:128, sp * 512:(sp + 1) * 512], rcp[64:128, 512:1024]))
            S.wait("tensor", VM)
            S.wait("tensor", D_G)
            for tl in range(8):
                i = H * 8 + tl
                for n in range(2):
                    for k in range(DC):
                        TE(lambda t, k=k, n=n, tl=tl: t.matmul(
                            psC[:, n * 512:(n + 1) * 512],
                            ofm[:, k * 1024 + tl * 128:k * 1024 + (tl + 1) * 128],
                            wbig[:, k * D + n * 512:k * D + n * 512 + 512],
                            start=(k == 0), stop=(k == DC - 1)),
                           TM if (k == DC - 1 and n == 1) else None)
                S.wait("vector", TM)
                V(lambda v, i=i: v.tensor_add(xt(i), xt(i), bobc[:]))
                scl = 0.5 if l == 1 else 1.0
                V(lambda v, i=i, scl=scl: v.scalar_tensor_tensor(
                    xt(i), psC[:, 0:1024], scl, xt(i), ALU.mult, ALU.add))
                S.wait("tensor", VM)

    subs = []
    for l in range(2):
        subs += [(l, "Fa"), (l, "At"), (l, "Fb")]
    subs = subs[:n_sub]

    try:
      for seq in range(SEQS):
          if seq > 0:
              S.wait("sync", VM)
              S.dma("sync", y_d[seq - 1].rearrange("(i p) d -> p i d", p=128),
                    x_sb[:].rearrange("p (i d) -> p i d", d=D), ST)
              S.wait("sync", ST)
              S.dma("sync", x_sb[:].rearrange("p (i d) -> p i d", d=D),
                    x_d[seq].rearrange("(i p) d -> p i d", p=128), D_S)
          S.wait("vector", D_S)
          S.wait("scalar", D_S)
          S.wait("tensor", D_S)
          S.wait("vector", D_G)
          for (l, kind) in subs:
              with ExitStack() as pes:
                  if kind == "At":
                      attn_phase(l, seq, pes)
                  else:
                      ffn_phase(l, seq, 0 if kind == "Fa" else 1, pes)
                  S.wait("vector", TM)
                  S.wait("vector", SM)
                  S.wait("tensor", VM)
                  S.wait("tensor", SM)
                  S.wait("scalar", VM)
                  S.wait("scalar", TM)
                  S.wait("gpsimd", VM)
                  S.wait("sync", VM)
      S.wait("sync", VM)
      S.dma("sync", y_d[SEQS - 1].rearrange("(i p) d -> p i d", p=128),
              x_sb[:].rearrange("p (i d) -> p i d", d=D), ST)
      S.wait("sync", ST)
    except _StopEmit:
        pass

    with nc.Block() as block:
        S.build(block)
    return nc, es


def kernel(**inputs):
    host = _prep(inputs)
    x = np.ascontiguousarray(np.asarray(inputs["L0_attn_x"], dtype=np.float32))
    nc, es = build_program(N_SUBLAYERS, DEBUG_TAPS)
    in_maps = []
    for c in range(NCORES := 8):
        m = dict(host)
        m["x"] = np.ascontiguousarray(x[2 * c:2 * c + 2])
        in_maps.append(m)
    res = run_bass_kernel_spmd(nc, in_maps, core_ids=list(range(8)))
    es.close()
    return np.concatenate([np.asarray(res.results[c]["y"]) for c in range(8)],
                          axis=0).astype(np.float32)


# version 02
# speedup vs baseline: 1.2083x; 1.2083x over previous
import numpy as np
import ml_dtypes
from contextlib import ExitStack

import concourse.bass as bass
import concourse.mybir as mybir
from concourse.bass_utils import run_bass_kernel_spmd

F32 = mybir.dt.float32
BF16 = mybir.dt.bfloat16
FP8 = mybir.dt.float8e4
I32 = mybir.dt.int32
AF = mybir.ActivationFunctionType
ALU = mybir.AluOpType
AX = mybir.AxisListType.X

B, T, D, NH, HD, HF = 16, 2048, 1024, 16, 64, 4096
NT, DC, HC = T // 128, D // 128, HF // 128
SEQS = 2
L0_WIN = 4

N_SUBLAYERS = 6
DEBUG_STOP = False
DEBUG_TAPS = False


class _StopEmit(Exception):
    pass


class Sched:
    ENGINES = ("sync", "gpsimd", "tensor", "vector", "scalar")

    def __init__(self, nc, es):
        self.nc = nc
        self.es = es
        self.items = {e: [] for e in self.ENGINES}
        self.sems = {}
        self.cnt = {}

    def sem(self, name):
        if name not in self.sems:
            self.sems[name] = self.es.enter_context(self.nc.semaphore(name))
            self.cnt[name] = 0
        return self.sems[name]

    def emit(self, eng, fn, inc=None, k=1):
        if inc is not None:
            self.sem(inc)
            self.cnt[inc] += k
        self.items[eng].append(("op", fn, inc, k))

    def dma(self, eng, out_ap, in_ap, inc, slow=False):
        self.sem(inc)
        self.cnt[inc] += 16
        self.items[eng].append(("dma", (out_ap, in_ap, slow), inc, 16))

    def wait(self, eng, semname, val=None):
        self.sem(semname)
        if val is None:
            val = self.cnt[semname]
        assert val <= self.cnt[semname]
        if val > 0:
            self.items[eng].append(("wait", semname, val, 0))

    def val(self, semname):
        self.sem(semname)
        return self.cnt[semname]

    def build(self, block):
        def mk(eng_name):
            def run(engine):
                for kind, a, bb, k in self.items[eng_name]:
                    if kind == "wait":
                        engine.wait_ge(self.sems[a], bb)
                    elif kind == "dma":
                        engine.dma_start(out=a[0], in_=a[1],
                                         allow_slow_non_contiguous=a[2]
                                         ).then_inc(self.sems[bb], 16)
                    else:
                        inst = a(engine)
                        if bb is not None:
                            inst.then_inc(self.sems[bb], k)
            return run
        block.sync(mk("sync"))
        block.gpsimd(mk("gpsimd"))
        block.tensor(mk("tensor"))
        block.vector(mk("vector"))
        block.scalar(mk("scalar"))


def _prep(inputs):
    t = lambda a: np.ascontiguousarray(np.asarray(a).T)
    d = {}
    for l, p in ((0, "L0"), (1, "L1")):
        for w, n in (("wqt", "Wq"), ("wkt", "Wk"), ("wvt", "Wv"), ("wot", "Wo")):
            d[f"{w}{l}"] = t(inputs[f"{p}_attn_{n}"])
        for b in ("bq", "bk", "bv", "bo"):
            d[f"{b}{l}"] = np.ascontiguousarray(inputs[f"{p}_attn_{b}"])
    d["w1t0"] = t(inputs["L0_ffn_W1"]); d["w2t0"] = t(inputs["L0_ffn_W2"])
    d["b1_0"] = np.ascontiguousarray(inputs["L0_ffn_b1"])
    d["b2_0"] = np.ascontiguousarray(inputs["L0_ffn_b2"])
    d["w1t1"] = t(inputs["L1_ffn_w1"]); d["w2t1"] = t(inputs["L1_ffn_w2"])
    d["n1w1"] = np.ascontiguousarray(inputs["L1_ffn_norm1_w"])
    d["n2w1"] = np.ascontiguousarray(inputs["L1_ffn_norm2_w"])
    d["asc1"] = np.ascontiguousarray(inputs["L1_ffn_act_scale"]).reshape(1)
    d["abi1"] = np.ascontiguousarray(inputs["L1_ffn_act_bias"]).reshape(1)
    d["ident"] = np.eye(128, dtype=ml_dtypes.bfloat16)
    e2 = np.zeros((128, 2), np.float32)
    e2[0:64, 0] = 1.0
    e2[64:128, 1] = 1.0
    d["e2"] = e2.astype(ml_dtypes.bfloat16)
    d["e2t"] = np.ascontiguousarray(e2.T)
    bj = np.arange(128, dtype=np.float32)
    d["biasv"] = np.ascontiguousarray(
        np.stack([bj - 64.0 - 128.0 * z for z in range(L0_WIN)], axis=1))
    d["tri"] = (np.arange(128)[:, None] <= np.arange(128)[None, :]
                ).astype(ml_dtypes.bfloat16)
    d["epsv"] = np.broadcast_to(np.array([1e-5, 1e-8], np.float32), (128, 2)).copy()
    d["ones1"] = np.ones((1, 128), np.float32)
    d["ones128"] = np.ones((128, 1), np.float32)
    d["onesr"] = np.ones((65, 128), np.float32)
    d["ones128b"] = np.ones((128, 1), ml_dtypes.bfloat16)
    return d


def build_program(n_sub=6, DEBUG_TAPS=False, DEBUG_STOP=False):
    nc = bass.Bass("TRN2", target_bir_lowering=False,
                   detect_race_conditions=False)
    es = ExitStack()
    dram = {}

    def din(name, shape, dtype=F32):
        dram[name] = nc.dram_tensor(name, list(shape), dtype, kind="ExternalInput").ap()

    din("x", [SEQS, T, D])
    for l in range(2):
        for w in ("wqt", "wkt", "wvt", "wot"):
            din(f"{w}{l}", [D, D])
        for b in ("bq", "bk", "bv", "bo"):
            din(f"{b}{l}", [D])
    din("w1t0", [D, HF]); din("w2t0", [HF, D])
    din("b1_0", [HF]); din("b2_0", [D])
    din("w1t1", [D, HF]); din("w2t1", [HF, D])
    din("n1w1", [D]); din("n2w1", [HF])
    din("asc1", [1]); din("abi1", [1])
    din("ident", [128, 128], BF16)
    din("e2", [128, 2], BF16); din("e2t", [2, 128])
    din("biasv", [128, L0_WIN])
    din("tri", [128, 128], BF16)
    din("ones1", [1, 128]); din("ones128", [128, 1]); din("epsv", [128, 2])
    din("onesr", [65, 128]); din("ones128b", [128, 1], BF16)
    x_d = dram["x"]
    y_d = nc.dram_tensor("y", [SEQS, T, D], F32, kind="ExternalOutput").ap()
    taps = {}
    if DEBUG_TAPS:
        for nm, sh, dt in (("t_xq", [2, 128, D], BF16), ("t_h", [2, 128, HF], F32),
                           ("t_hq", [2, 128, HF], BF16), ("t_stt", [2, 128, 16], F32),
                           ("t_w1q", [128, DC * HF], FP8), ("t_w2q", [128, HC * D], FP8),
                           ("t_ws", [128, 4], F32), ("t_m", [2, 128, 1], F32), ("t_b1", [128, HF], F32),
                           ("t_r", [2, 128, 1], F32), ("t_vv", [2, 128, 1], F32)):
            taps[nm] = nc.dram_tensor(nm, sh, dt, kind="ExternalOutput").ap()
    wscr = {}
    for l in range(2):
        wscr[(l, 1)] = nc.dram_tensor(f"w1scr{l}", [D, HF], FP8, kind="Internal").ap()
        wscr[(l, 2)] = nc.dram_tensor(f"w2scr{l}", [HF, D], FP8, kind="Internal").ap()

    sbt = lambda name, shape, dtype=F32: es.enter_context(
        nc.sbuf_tensor(name, list(shape), dtype))
    x_sb = sbt("x_sb", [128, NT * D])
    ident = sbt("ident_sb", [128, 128], BF16)
    e2_sb = sbt("e2_sb", [128, 2], BF16)
    e2t_sb = sbt("e2t_sb", [2, 128])
    biasv_sb = sbt("biasv_sb", [128, L0_WIN])
    tri_sb = sbt("tri_sb", [128, 128], BF16)
    ones1_sb = sbt("ones1_sb", [1, 128])
    eps_sb = sbt("eps_sb", [128, 2])
    ones128_sb = sbt("ones128_sb", [128, 1])
    onesr_sb = sbt("onesr_sb", [65, 128])
    ones128b_sb = sbt("ones128b_sb", [128, 1], BF16)
    ws_sb = sbt("ws_sb", [128, 4])
    ai_sb = sbt("ai_sb", [128, 2])
    st_m = sbt("st_m", [128, NT])
    st_r = sbt("st_r", [128, NT])
    st_r2 = sbt("st_r2", [128, NT])
    st_v = sbt("st_v", [128, NT])
    stt = sbt("stt", [128, 32])
    st2 = sbt("st2", [128, 8])
    spc = sbt("spc", [128, 4])
    _spk = [0]
    nt_a = sbt("nt_a", [128, NT])
    scr = {}
    psA = es.enter_context(nc.psum_tensor("psA", [128, 512], F32))
    psB = es.enter_context(nc.psum_tensor("psB", [128, 512], F32))
    psC = es.enter_context(nc.psum_tensor("psC", [128, 1024], F32))

    S = Sched(nc, es)
    D_S, D_G, TM, VM, SM, ST = "dS", "dG", "tm", "vm", "sm", "st"

    S.dma("sync", x_sb[:].rearrange("p (i d) -> p i d", d=D),
          x_d[0].rearrange("(i p) d -> p i d", p=128), D_S)
    for name, tile in (("ident", ident), ("e2t", e2t_sb), ("biasv", biasv_sb),
                       ("tri", tri_sb), ("ones1", ones1_sb), ("ones128", ones128_sb), ("epsv", eps_sb),
                       ("onesr", onesr_sb), ("ones128b", ones128b_sb)):
        S.dma("sync", tile[:], dram[name][:, :], D_S)
    S.dma("gpsimd", e2_sb[:], dram["e2"][:, :], D_G)
    S.dma("gpsimd", ai_sb[:, 0:1], dram["asc1"][None, :].broadcast_to((128, 1)), D_G)
    S.dma("gpsimd", ai_sb[:, 1:2], dram["abi1"][None, :].broadcast_to((128, 1)), D_G)
    S.emit("vector", lambda v: v.memset(spc[:], 0.0), "vm")
    S.emit("vector", lambda v: v.memset(nt_a[:], 0.0), "vm")

    def V(fn, inc=VM):
        S.emit("vector", fn, inc)

    def _spacer(inc):
        k = _spk[0] % 4
        _spk[0] += 1
        S.emit("vector", lambda v, k=k: v.tensor_copy(spc[:, k:k + 1],
                                                      eps_sb[:, 0:1]), inc)

    def Vt(fn):
        S.emit("vector", fn, None)
        _spacer(VM)

    def Vred(fn):
        S.emit("vector", fn, VM)
        _spacer(None)
        _spacer(VM)
        S.wait("vector", VM)

    def SC(fn, inc=SM):
        S.emit("scalar", fn, inc)

    def TE(fn, inc=None):
        S.emit("tensor", fn, inc)

    def xt(i):
        return x_sb[:, i * D:(i + 1) * D]

    def ternarize(tes, wt_d, R, C, out_sb, ws_col, stash_d):
        nsl = R // 128
        sc = lambda n, sh, dt=F32: tes.enter_context(nc.sbuf_tensor(n, list(sh), dt))
        psW = tes.enter_context(nc.psum_tensor(f"psW{ws_col}", [128, 512], F32))
        stg = [sc(f"tg{ws_col}a", [128, C]), sc(f"tg{ws_col}b", [128, C])]
        pacc = sc(f"pc{ws_col}", [128, nsl])
        ab = sc(f"ta{ws_col}", [128, 2048])
        chans = ("twA", "twB")
        for s in range(nsl):
            buf = stg[s % 2]
            ch = chans[s % 2]
            S.wait("gpsimd", VM)
            S.dma("gpsimd", buf[:], wt_d[s * 128:(s + 1) * 128, :], ch)
            S.wait("vector", ch)
            Vred(lambda v, b=buf, s=s: v.tensor_reduce(
                pacc[:, s:s + 1], b[:], AX, ALU.add, apply_absolute_value=True))
        Vred(lambda v, nsl=nsl: v.tensor_reduce(stt[:, 0:1], pacc[:, 0:nsl], AX, ALU.add))
        _spacer(VM)
        S.wait("tensor", VM)
        TE(lambda t: t.matmul(psW[0:1, 0:1], ones128_sb[:], stt[:, 0:1],
                              start=True, stop=True), TM)
        S.wait("vector", TM)
        Vt(lambda v: v.tensor_copy(stt[0:1, 1:2], psW[0:1, 0:1]))
        _spacer(VM)
        S.wait("tensor", VM)
        TE(lambda t: t.matmul(psW[:, 4:5], ones1_sb[:], stt[0:1, 1:2],
                              start=True, stop=True), TM)
        S.wait("vector", TM)
        mmul = 1.0 / float(R * C)
        Vt(lambda v: v.tensor_scalar(stt[:, 2:3], psW[:, 4:5], mmul, None, ALU.mult))
        Vt(lambda v, w=ws_col: v.tensor_scalar(ws_sb[:, w:w + 1], stt[:, 2:3],
                                               1e-5, None, ALU.max))
        Vt(lambda v, w=ws_col: v.tensor_scalar(stt[:, 3:4], ws_sb[:, w:w + 1],
                                               0.5, None, ALU.mult))
        Vt(lambda v, w=ws_col: v.tensor_scalar(stt[:, 4:5], ws_sb[:, w:w + 1],
                                               -0.5, None, ALU.mult))
        for s in range(nsl):
            buf = stg[s % 2]
            ch = chans[s % 2]
            S.wait("gpsimd", VM)
            S.dma("gpsimd", buf[:], wt_d[s * 128:(s + 1) * 128, :], ch)
            S.wait("vector", ch)
            for j in range(0, C, 1024):
                w = min(1024, C - j)
                V(lambda v, b=buf, j=j, w=w: v.tensor_scalar(
                    ab[:, 0:w], b[:, j:j + w], stt[:, 3:4], None, ALU.is_ge), None)
                V(lambda v, b=buf, j=j, w=w: v.tensor_scalar(
                    ab[:, 1024:1024 + w], b[:, j:j + w], stt[:, 4:5], None,
                    ALU.is_le), None)
                V(lambda v, s=s, j=j, w=w: v.tensor_sub(
                    out_sb[:, s * C + j:s * C + j + w], ab[:, 0:w],
                    ab[:, 1024:1024 + w]))
        S.wait("gpsimd", VM)
        S.dma("gpsimd", stash_d.rearrange("(s p) c -> p s c", p=128),
              out_sb[:, 0:nsl * C].rearrange("p (s c) -> p s c", c=C), "twS")

    def load_fp8(stash_d, out_sb, nsl, C):
        S.wait("gpsimd", "twS")
        S.dma("gpsimd", out_sb[:, 0:nsl * C].rearrange("p (s c) -> p s c", c=C),
              stash_d.rearrange("(s p) c -> p s c", p=128), D_G)

    def ln_prepass(want_r2, junk):
        for i in range(NT):
            Vred(lambda v, i=i: v.reduce_sum(stt[:, 0:1], xt(i), AX))
            V(lambda v, i=i: v.tensor_scalar(st_m[:, i:i + 1], stt[:, 0:1],
                                             1.0 / D, None, ALU.mult))
            S.wait("scalar", VM)
            SC(lambda s, i=i, junk=junk: s.activation(
                junk[:, 0:D], xt(i), AF.Square, accum_out=stt[:, 1:2]))
            S.wait("vector", SM)
            Vt(lambda v: v.tensor_scalar(stt[:, 2:3], stt[:, 1:2], 1.0 / D,
                                         None, ALU.mult))
            Vt(lambda v, i=i: v.tensor_mul(stt[:, 3:4], st_m[:, i:i + 1],
                                           st_m[:, i:i + 1]))
            Vt(lambda v, i=i: v.tensor_sub(st_v[:, i:i + 1], stt[:, 2:3],
                                           stt[:, 3:4]))
        Vt(lambda v: v.tensor_scalar(st_v[:], st_v[:], 1e-5, None, ALU.add))
        S.wait("scalar", VM)
        SC(lambda s: s.activation(st_r[:], st_v[:], AF.Ln, bias=0.0, scale=1.0))
        SC(lambda s: s.activation(st_r[:], st_r[:], AF.Exp, bias=0.0, scale=-0.5))
        S.wait("vector", SM)
        newton_rsqrt(st_r[:], st_v[:], NT)
        if want_r2:
            Vt(lambda v: v.tensor_mul(st_r2[:], st_r[:], st_r[:]))
            Vt(lambda v: v.tensor_scalar(st_v[:], st_v[:], -1e-5, None, ALU.add))
            Vt(lambda v: v.tensor_mul(st_r2[:], st_r2[:], st_v[:]))
            Vt(lambda v: v.tensor_scalar(st_r2[:], st_r2[:], 1e-8, None, ALU.add))
            Vt(lambda v: v.tensor_copy(st_v[:], st_r2[:]))
            S.wait("scalar", VM)
            SC(lambda s: s.activation(st_r2[:], st_r2[:], AF.Ln, bias=0.0, scale=1.0))
            SC(lambda s: s.activation(st_r2[:], st_r2[:], AF.Exp, bias=0.0, scale=-0.5))
            S.wait("vector", SM)
            newton_rsqrt(st_r2[:], st_v[:], NT)
        S.wait("vector", SM)

    def newton_rsqrt(rt, vt2, W):
        Vt(lambda v: v.tensor_mul(nt_a[:, 0:W], rt, rt))
        Vt(lambda v: v.tensor_mul(nt_a[:, 0:W], nt_a[:, 0:W], vt2))
        Vt(lambda v: v.tensor_scalar(nt_a[:, 0:W], nt_a[:, 0:W], -0.5, 1.5,
                                     ALU.mult, ALU.add))
        Vt(lambda v: v.tensor_mul(rt, rt, nt_a[:, 0:W]))

    def ffn_phase(l, seq, half, pes):
        sc = lambda n, sh, dt=F32: pes.enter_context(nc.sbuf_tensor(n, list(sh), dt))
        tag = f"f{l}{seq}{half}"
        w1q = sc(f"w1q{tag}", [128, DC * HF], FP8)
        w2q = sc(f"w2q{tag}", [128, HC * D], FP8)

        if seq == 0 and half == 0:
            with ExitStack() as tes:
                ternarize(tes, dram[f"w1t{l}"], D, HF, w1q, 2 * l, wscr[(l, 1)])
                ternarize(tes, dram[f"w2t{l}"], HF, D, w2q, 2 * l + 1, wscr[(l, 2)])
        else:
            load_fp8(wscr[(l, 1)], w1q, DC, HF)
            load_fp8(wscr[(l, 2)], w2q, HC, D)

        psTx = pes.enter_context(nc.psum_tensor(f"psTx{tag}", [128, 1024], BF16))
        psTh = [pes.enter_context(nc.psum_tensor(f"psTh{t}{tag}", [128, 1024],
                                                 BF16)) for t in range(2)]
        tbuf = [sc(f"tb{p}{tag}", [128, D]) for p in range(2)]
        xq_tm = [sc(f"xq{p}{tag}", [128, D], BF16) for p in range(2)]
        xqT = sc(f"xqT{tag}", [128, D], BF16)
        qx = sc(f"qx{tag}", [128, 1024], I32)
        qh = sc(f"qh{tag}", [128, 1024], I32)
        debf = sc(f"debf{tag}", [128, 1024])
        h_sb = sc(f"h{tag}", [128, HF])
        hq_tm = sc(f"hq{tag}", [128, HF], BF16)
        hqT = sc(f"hqT{tag}", [128, 2048], BF16)
        if l == 0:
            b1bc = sc(f"b1bc{tag}", [128, HF])
            b2bc = sc(f"b2bc{tag}", [128, D])
            S.wait("sync", VM)
            S.dma("sync", b1bc[:], dram["b1_0"][None, :].broadcast_to((128, HF)), D_S)
            S.dma("sync", b2bc[:], dram["b2_0"][None, :].broadcast_to((128, D)), D_S)
        else:
            n1bc = sc(f"n1bc{tag}", [128, D])
            n2bc = sc(f"n2bc{tag}", [128, HF])
            S.wait("sync", VM)
            S.dma("sync", n1bc[:], dram["n1w1"][None, :].broadcast_to((128, D)), D_S)
            S.dma("sync", n2bc[:], dram["n2w1"][None, :].broadcast_to((128, HF)), D_S)

        ln_prepass(want_r2=(l == 1), junk=xq_tm[0])
        S.wait("vector", D_S)
        S.wait("scalar", D_S)
        S.wait("tensor", D_G)
        S.wait("vector", D_G)

        vm_xq = [0] * NT; vm_s10 = [0] * NT; vm_scq = [0] * NT
        tm_t1 = [0] * NT; sm_c1 = [0] * NT
        tm_f1 = [[0] * 8 for _ in range(NT)]
        bankfree = [[0] * 8 for _ in range(NT)]
        bf_sem = VM if l == 0 else SM
        sm_act7 = [0] * NT
        vm_hqs = [[0] * 8 for _ in range(NT)]
        tm_t2r = [[0] * 4 for _ in range(NT)]
        sm_c2 = [[0] * 4 for _ in range(NT)]
        tm_f2g = [[0] * 4 for _ in range(NT)]
        vm_res = [0] * NT

        def chain3(src, s_ap, qreg, dst):
            V(lambda v: v.scalar_tensor_tensor(debf[:, 0:512], src, s_ap, qreg,
                                               ALU.mult, ALU.subtract))
            V(lambda v: v.scalar_tensor_tensor(debf[:, 512:1024], debf[:, 0:512],
                                               1.5, qreg, ALU.is_gt, ALU.add))
            V(lambda v: v.scalar_tensor_tensor(dst, debf[:, 0:512], 0.5,
                                               debf[:, 512:1024], ALU.is_ge,
                                               ALU.add))

        def emit_P(i):
            p = i % 2; po = 16 * p
            m_ap = st_m[:, i:i + 1]
            if l == 0:
                V(lambda v, i=i, m=m_ap, p=p: v.tensor_scalar(
                    tbuf[p][:, 0:D], xt(i), m, None, ALU.subtract))
            else:
                V(lambda v, i=i, m=m_ap, p=p: v.scalar_tensor_tensor(
                    tbuf[p][:, 0:D], xt(i), m, n1bc[:], ALU.subtract, ALU.mult))
            Vred(lambda v, p=p, po=po: v.tensor_reduce(
                stt[:, po + 5:po + 6], tbuf[p][:, 0:D], AX, ALU.max,
                apply_absolute_value=True))
            if l == 0:
                Vt(lambda v, i=i, po=po: v.tensor_copy(stt[:, po + 9:po + 10],
                                                       st_r[:, i:i + 1]))
            else:
                Vt(lambda v, i=i, po=po: v.tensor_mul(
                    stt[:, po + 9:po + 10], st_r[:, i:i + 1], st_r2[:, i:i + 1]))
            Vt(lambda v, po=po: v.tensor_mul(stt[:, po + 6:po + 7],
                                             stt[:, po + 5:po + 6],
                                             stt[:, po + 9:po + 10]))
            Vt(lambda v, po=po: v.tensor_scalar(stt[:, po + 6:po + 7],
                                                stt[:, po + 6:po + 7], 1e-5,
                                                None, ALU.max))
            Vt(lambda v, po=po: v.reciprocal(stt[:, po + 7:po + 8],
                                             stt[:, po + 6:po + 7]))
            Vt(lambda v, po=po: v.tensor_scalar(stt[:, po + 7:po + 8],
                                                stt[:, po + 7:po + 8], 127.0,
                                                None, ALU.mult))
            Vt(lambda v, po=po: v.tensor_mul(stt[:, po + 8:po + 9],
                                             stt[:, po + 7:po + 8],
                                             stt[:, po + 9:po + 10]))
            Vt(lambda v, po=po: v.tensor_scalar(stt[:, po + 10:po + 11],
                                                stt[:, po + 6:po + 7],
                                                1.0 / 127.0, None, ALU.mult))
            Vt(lambda v, po=po, l=l: v.tensor_mul(stt[:, po + 10:po + 11],
                                                  stt[:, po + 10:po + 11],
                                                  ws_sb[:, 2 * l:2 * l + 1]))
            _spacer(VM)
            vm_s10[i] = S.val(VM)
            for sl in range(2):
                qreg = qx[:, sl * 512:(sl + 1) * 512]
                if sl == 0:
                    S.wait("scalar", VM, vm_s10[i])
                SC(lambda s, p=p, po=po, sl=sl, qreg=qreg: s.activation(
                    qreg, tbuf[p][:, sl * 512:(sl + 1) * 512], AF.Copy,
                    bias=-1.0, scale=stt[:, po + 8:po + 9]))
                smv = S.val(SM)
                S.wait("vector", SM, smv)
                chain3(tbuf[p][:, sl * 512:(sl + 1) * 512],
                       stt[:, po + 8:po + 9], qreg,
                       xq_tm[p][:, sl * 512:(sl + 1) * 512])
            vm_xq[i] = S.val(VM)

        def emit_T1(i):
            p = i % 2
            if i > 0:
                S.wait("tensor", SM, sm_c1[i - 1])
            S.wait("tensor", VM, vm_xq[i])
            for k in range(DC):
                TE(lambda t, k=k, p=p: t.transpose(
                    psTx[:, k * 128:(k + 1) * 128],
                    xq_tm[p][:, k * 128:(k + 1) * 128], ident[:]),
                   TM if k == DC - 1 else None)
            tm_t1[i] = S.val(TM)

        def emit_C1(i):
            wv = tm_t1[i] if i == 0 else max(tm_t1[i], tm_f1[i - 1][7])
            S.wait("scalar", TM, wv)
            SC(lambda s: s.activation(xqT[:, 0:D], psTx[:, 0:D], AF.Copy,
                                      bias=0.0, scale=1.0))
            sm_c1[i] = S.val(SM)

        def emit_F1(i):
            S.wait("tensor", SM, sm_c1[i])
            for n in range(8):
                pb = psA if n % 2 == 0 else psB
                if i > 0:
                    S.wait("tensor", bf_sem, bankfree[i - 1][n])
                for k in range(DC):
                    TE(lambda t, k=k, n=n, pb=pb: t.matmul(
                        pb[:, 0:512], xqT[:, k * 128:(k + 1) * 128],
                        w1q[:, k * HF + n * 512:k * HF + n * 512 + 512],
                        start=(k == 0), stop=(k == DC - 1)),
                       TM if k == DC - 1 else None)
                tm_f1[i][n] = S.val(TM)

        def emit_E1(i):
            p = i % 2; po = 16 * p
            if l == 0:
                for n in range(8):
                    pb = psA if n % 2 == 0 else psB
                    S.wait("vector", TM, tm_f1[i][n])
                    V(lambda v, n=n, pb=pb, po=po: v.scalar_tensor_tensor(
                        h_sb[:, n * 512:(n + 1) * 512], pb[:, 0:512],
                        stt[:, po + 10:po + 11], b1bc[:, n * 512:(n + 1) * 512],
                        ALU.mult, ALU.add))
                    bankfree[i][n] = S.val(VM)
                    S.wait("scalar", VM, bankfree[i][n])
                    SC(lambda s, n=n: s.activation(
                        h_sb[:, n * 512:(n + 1) * 512],
                        h_sb[:, n * 512:(n + 1) * 512], AF.Gelu))
                sm_act7[i] = S.val(SM)
            else:
                S.wait("scalar", VM, vm_s10[i])
                for n in range(8):
                    pb = psA if n % 2 == 0 else psB
                    S.wait("scalar", TM, tm_f1[i][n])
                    SC(lambda s, n=n, pb=pb, po=po: s.activation(
                        h_sb[:, n * 512:(n + 1) * 512], pb[:, 0:512], AF.Relu,
                        bias=0.0, scale=stt[:, po + 10:po + 11]))
                    bankfree[i][n] = S.val(SM)
                    SC(lambda s, n=n: s.activation(
                        h_sb[:, n * 512:(n + 1) * 512],
                        h_sb[:, n * 512:(n + 1) * 512], AF.Square))
                    smv = S.val(SM)
                    S.wait("vector", SM, smv)
                    V(lambda v, n=n: v.tensor_scalar(
                        h_sb[:, n * 512:(n + 1) * 512],
                        h_sb[:, n * 512:(n + 1) * 512],
                        ai_sb[:, 0:1], ai_sb[:, 1:2], ALU.mult, ALU.add))

        def emit_rms2(i):
            p = i % 2; po = 16 * p
            vm_ts = S.val(VM)
            S.wait("scalar", VM, vm_ts)
            for j4 in range(4):
                SC(lambda s, j4=j4, p=p: s.activation(
                    xq_tm[1 - p][:, 0:1024], h_sb[:, j4 * 1024:(j4 + 1) * 1024],
                    AF.Square, accum_out=st2[:, 4 * p + j4:4 * p + j4 + 1]))
            smv = S.val(SM)
            S.wait("vector", SM, smv)
            Vred(lambda v, p=p, po=po: v.tensor_reduce(
                stt[:, po + 11:po + 12], st2[:, 4 * p:4 * p + 4], AX, ALU.add))
            Vt(lambda v, po=po: v.tensor_scalar(
                stt[:, po + 11:po + 12], stt[:, po + 11:po + 12],
                1.0 / HF, 1e-8, ALU.mult, ALU.add))
            Vt(lambda v, po=po: v.tensor_copy(stt[:, po + 15:po + 16],
                                              stt[:, po + 11:po + 12]))
            _spacer(VM)
            vmv = S.val(VM)
            S.wait("scalar", VM, vmv)
            SC(lambda s, po=po: s.activation(stt[:, po + 11:po + 12],
                                             stt[:, po + 11:po + 12], AF.Ln,
                                             bias=0.0, scale=1.0))
            SC(lambda s, po=po: s.activation(stt[:, po + 11:po + 12],
                                             stt[:, po + 11:po + 12], AF.Exp,
                                             bias=0.0, scale=-0.5))
            smv = S.val(SM)
            S.wait("vector", SM, smv)
            newton_rsqrt(stt[:, po + 11:po + 12], stt[:, po + 15:po + 16], 1)
            V(lambda v, po=po: v.scalar_tensor_tensor(
                h_sb[:, 0:HF], h_sb[:, 0:HF], stt[:, po + 11:po + 12], n2bc[:],
                ALU.mult, ALU.mult))

        def emit_Q2(i):
            p = i % 2; po = 16 * p
            if l == 0:
                S.wait("vector", SM, sm_act7[i])
            Vred(lambda v, po=po: v.tensor_reduce(
                stt[:, po + 12:po + 13], h_sb[:, 0:HF], AX, ALU.max,
                apply_absolute_value=True))
            Vt(lambda v, po=po: v.tensor_scalar(stt[:, po + 12:po + 13],
                                                stt[:, po + 12:po + 13], 1e-5,
                                                None, ALU.max))
            Vt(lambda v, po=po: v.reciprocal(stt[:, po + 13:po + 14],
                                             stt[:, po + 12:po + 13]))
            Vt(lambda v, po=po: v.tensor_scalar(stt[:, po + 13:po + 14],
                                                stt[:, po + 13:po + 14], 127.0,
                                                None, ALU.mult))
            Vt(lambda v, po=po: v.tensor_scalar(stt[:, po + 14:po + 15],
                                                stt[:, po + 12:po + 13],
                                                0.5 / 127.0, None, ALU.mult))
            Vt(lambda v, po=po, l=l: v.tensor_mul(stt[:, po + 14:po + 15],
                                                  stt[:, po + 14:po + 15],
                                                  ws_sb[:, 2 * l + 1:2 * l + 2]))
            _spacer(VM)
            vm_scq[i] = S.val(VM)
            for sl in range(8):
                qreg = qh[:, (sl % 2) * 512:(sl % 2) * 512 + 512]
                wv = vm_scq[i] if sl < 2 else vm_hqs[i][sl - 2]
                S.wait("scalar", VM, wv)
                SC(lambda s, sl=sl, po=po, qreg=qreg: s.activation(
                    qreg, h_sb[:, sl * 512:(sl + 1) * 512], AF.Copy, bias=-1.0,
                    scale=stt[:, po + 13:po + 14]))
                smv = S.val(SM)
                S.wait("vector", SM, smv)
                if sl == 0 and i > 0:
                    S.wait("vector", TM, tm_t2r[i - 1][3])
                chain3(h_sb[:, sl * 512:(sl + 1) * 512],
                       stt[:, po + 13:po + 14], qreg,
                       hq_tm[:, sl * 512:(sl + 1) * 512])
                vm_hqs[i][sl] = S.val(VM)

        def _t2round(j, rr):
            S.wait("tensor", VM, vm_hqs[j][2 * rr + 1])
            if rr >= 2:
                S.wait("tensor", SM, sm_c2[j][rr - 2])
            for k in range(8):
                TE(lambda t, k=k, rr=rr: t.transpose(
                    psTh[rr % 2][:, k * 128:(k + 1) * 128],
                    hq_tm[:, rr * 1024 + k * 128:rr * 1024 + (k + 1) * 128],
                    ident[:]), TM if k == 7 else None)
            tm_t2r[j][rr] = S.val(TM)

        def _c2(j, rr):
            wv = tm_t2r[j][rr]
            if rr >= 2:
                wv = max(wv, tm_f2g[j][rr - 2])
            elif j > 0:
                wv = max(wv, tm_f2g[j - 1][2 + rr])
            S.wait("scalar", TM, wv)
            SC(lambda s, rr=rr: s.activation(
                hqT[:, (rr % 2) * 1024:(rr % 2) * 1024 + 1024],
                psTh[rr % 2][:, 0:1024], AF.Copy, bias=0.0, scale=1.0))
            sm_c2[j][rr] = S.val(SM)

        def _f2group(j, g):
            S.wait("tensor", SM, sm_c2[j][g])
            if g == 0 and j > 0:
                S.wait("tensor", VM, vm_res[j - 1])
            for k in range(8):
                kk = g * 8 + k
                for n in range(2):
                    TE(lambda t, k=k, kk=kk, n=n, g=g: t.matmul(
                        psC[:, n * 512:(n + 1) * 512],
                        hqT[:, (g % 2) * 1024 + k * 128:
                            (g % 2) * 1024 + (k + 1) * 128],
                        w2q[:, kk * D + n * 512:kk * D + n * 512 + 512],
                        start=(kk == 0), stop=(kk == HC - 1)),
                       TM if (k == 7 and n == 1) else None)
            tm_f2g[j][g] = S.val(TM)

        def emit_T2F2C2(j):
            _t2round(j, 0)
            _c2(j, 0)
            _t2round(j, 1)
            _c2(j, 1)
            _f2group(j, 0)
            _t2round(j, 2)
            _c2(j, 2)
            _f2group(j, 1)
            _t2round(j, 3)
            _c2(j, 3)
            _f2group(j, 2)
            _f2group(j, 3)

        def emit_R(j):
            po = 16 * (j % 2)
            S.wait("vector", TM, tm_f2g[j][3])
            if l == 0:
                V(lambda v, j=j: v.scalar_tensor_tensor(
                    xt(j), b2bc[:], 0.5, xt(j), ALU.mult, ALU.add))
            V(lambda v, j=j, po=po: v.scalar_tensor_tensor(
                xt(j), psC[:, 0:1024], stt[:, po + 14:po + 15], xt(j),
                ALU.mult, ALU.add))
            vm_res[j] = S.val(VM)

        emit_P(0)
        for i in range(NT + 1):
            if i < NT:
                emit_T1(i)
                emit_C1(i)
                emit_F1(i)
                emit_E1(i)
            if i >= 1:
                emit_T2F2C2(i - 1)
            if i < NT:
                if l == 1:
                    emit_rms2(i)
                emit_Q2(i)
            if i >= 1:
                emit_R(i - 1)
            if i + 1 < NT:
                emit_P(i + 1)

    def attn_phase(l, seq, pes):
        sc = lambda n, sh, dt=F32: pes.enter_context(nc.sbuf_tensor(n, list(sh), dt))
        tag = f"a{l}{seq}"
        psD = pes.enter_context(nc.psum_tensor(f"psD{tag}", [128, 1024], F32))
        psE = pes.enter_context(nc.psum_tensor(f"psE{tag}", [128, 512], F32))
        xhT = sc(f"xhT{tag}", [128, DC * T], BF16)
        vprime = sc(f"vp{tag}", [128, NT * NH * 65], BF16)
        wbig = sc(f"wbig{tag}", [128, DC * D], BF16)
        ofm = sc(f"ofm{tag}", [128, DC * 1024], BF16)
        qc_sb = sc(f"qc{tag}", [128, T], BF16)
        kc_sb = sc(f"kc{tag}", [128, T], BF16)
        wqk = sc(f"wqk{tag}", [128, DC * 256], BF16)
        pb0 = sc(f"pb0{tag}", [128, 1024], BF16)
        pb1 = sc(f"pb1{tag}", [128, 1024], BF16)
        rs_sb = sc(f"rs{tag}", [65, 1024])
        rcp = sc(f"rcp{tag}", [128, 1024])
        batt = sc(f"batt{tag}", [128, 4 * DC])
        bvbc = sc(f"bvbc{tag}", [128, D])
        bobc = sc(f"bobc{tag}", [128, D])
        qraw = sc(f"qraw{tag}", [128, 512])
        kraw = sc(f"kraw{tag}", [128, 512])
        sqb = sc(f"sqb{tag}", [128, 512], BF16)
        l2v = sc(f"l2v{tag}", [2, 512])
        xh_tm = sc(f"xh{tag}", [128, D], BF16)

        for w, b in ((0, "bq"), (1, "bk"), (2, "bv"), (3, "bo")):
            S.dma("sync", batt[:, w * DC:(w + 1) * DC],
                  dram[f"{b}{l}"].rearrange("(c p) -> p c", p=128), D_S, slow=True)
        S.dma("sync", bvbc[:], dram[f"bv{l}"][None, :].broadcast_to((128, D)), D_S)
        S.dma("sync", bobc[:], dram[f"bo{l}"][None, :].broadcast_to((128, D)), D_S)
        S.dma("gpsimd", wbig[:].rearrange("p (k o) -> p k o", o=D),
              dram[f"wvt{l}"].rearrange("(k p) o -> p k o", p=128), D_G)

        ln_prepass(want_r2=False, junk=xh_tm)
        S.wait("vector", D_S)
        if l == 1:
            V(lambda v: v.tensor_scalar(bobc[:], bobc[:], 0.5, None, ALU.mult))

        with ExitStack() as tscope:
            psT = tscope.enter_context(nc.psum_tensor(f"psT{tag}", [128, 1024], BF16))
            for i in range(NT):
                V(lambda v, i=i: v.tensor_scalar(
                    xh_tm[:], xt(i), st_m[:, i:i + 1], st_r[:, i:i + 1],
                    ALU.subtract, ALU.mult))
                S.wait("tensor", VM)
                for k in range(DC):
                    TE(lambda t, k=k, psT=psT: t.transpose(
                        psT[:, k * 128:(k + 1) * 128],
                        xh_tm[:, k * 128:(k + 1) * 128],
                        ident[:]), TM if k == DC - 1 else None)
                S.wait("vector", TM)
                V(lambda v, i=i, psT=psT: v.tensor_copy(
                    xhT[:].rearrange("p (k t) -> p k t", t=T)[:, :, i * 128:(i + 1) * 128],
                    psT[:, 0:D].rearrange("p (k c) -> p k c", c=128)))
        psR = pes.enter_context(nc.psum_tensor(f"psR{tag}", [128, 512], F32))

        S.wait("tensor", D_G)
        S.wait("tensor", VM)
        for i in range(NT):
            S.wait("tensor", VM)
            for n in range(2):
                pbn = psC if n == 0 else psD
                for k in range(DC):
                    TE(lambda t, i=i, n=n, k=k, pbn=pbn: t.matmul(
                        pbn[:, 0:512],
                        xhT[:, k * T + i * 128:k * T + (i + 1) * 128],
                        wbig[:, k * D + n * 512:k * D + n * 512 + 512],
                        start=(k == 0), stop=(k == DC - 1)),
                       TM if k == DC - 1 else None)
            S.wait("vector", TM)
            for n in range(2):
                pbn = psC if n == 0 else psD
                V(lambda v, i=i, n=n, pbn=pbn: v.tensor_tensor(
                    vprime[:, i * NH * 65:(i + 1) * NH * 65]
                    .rearrange("p (h c) -> p h c", c=65)[:, 8 * n:8 * n + 8, 0:64],
                    pbn[:, 0:512].rearrange("p (h c) -> p h c", c=64),
                    bvbc[:, n * 512:(n + 1) * 512]
                    .rearrange("p (h c) -> p h c", c=64), ALU.add))
            V(lambda v, i=i: v.memset(
                vprime[:, i * NH * 65:(i + 1) * NH * 65]
                .rearrange("p (h c) -> p h c", c=65)[:, :, 64:65], 1.0))
        S.wait("gpsimd", VM)
        S.dma("gpsimd", wbig[:].rearrange("p (k o) -> p k o", o=D),
              dram[f"wot{l}"].rearrange("(k p) o -> p k o", p=128), D_G)

        for H in range(2):
            qlo_t, qhi_t = H * 8, H * 8 + 8
            for c in range(8):
                S.wait("gpsimd", TM)
                S.wait("gpsimd", VM)
                S.dma("gpsimd", wqk[:, 0:DC * 128].rearrange("p (k o) -> p k o", o=128),
                      dram[f"wqt{l}"][:, c * 128:(c + 1) * 128]
                      .rearrange("(k p) o -> p k o", p=128), D_G)
                S.dma("gpsimd", wqk[:, DC * 128:].rearrange("p (k o) -> p k o", o=128),
                      dram[f"wkt{l}"][:, c * 128:(c + 1) * 128]
                      .rearrange("(k p) o -> p k o", p=128), D_G)
                S.wait("tensor", D_G)
                for qk in range(2):
                    dst = qc_sb if qk == 0 else kc_sb
                    raw = qraw if qk == 0 else kraw
                    wofs = qk * DC * 128
                    for s4 in range(4):
                        pbk = psA if s4 % 2 == 0 else psB
                        S.wait("tensor", VM)
                        S.wait("tensor", SM)
                        for k in range(DC):
                            TE(lambda t, k=k, s4=s4, pbk=pbk, wofs=wofs: t.matmul(
                                pbk[:, 0:512],
                                wqk[:, wofs + k * 128:wofs + (k + 1) * 128],
                                xhT[:, k * T + s4 * 512:k * T + (s4 + 1) * 512],
                                start=(k == 0), stop=(k == DC - 1)),
                               TM if k == DC - 1 else None)
                        if l == 0:
                            S.wait("vector", TM)
                            if qk == 0:
                                V(lambda v, s4=s4, pbk=pbk, dst=dst, c=c: v.tensor_scalar(
                                    dst[:, s4 * 512:(s4 + 1) * 512], pbk[:, 0:512],
                                    batt[:, c:c + 1], 0.125, ALU.add, ALU.mult))
                            else:
                                V(lambda v, s4=s4, pbk=pbk, dst=dst, c=c: v.tensor_scalar(
                                    dst[:, s4 * 512:(s4 + 1) * 512], pbk[:, 0:512],
                                    batt[:, DC + c:DC + c + 1], None, ALU.add))
                        else:
                            bcol = (0 if qk == 0 else DC) + c
                            S.wait("vector", TM)
                            V(lambda v, pbk=pbk, raw=raw, bcol=bcol: v.tensor_scalar(
                                raw[:], pbk[:, 0:512], batt[:, bcol:bcol + 1],
                                None, ALU.add))
                            V(lambda v, raw=raw: v.tensor_mul(sqb[:], raw[:], raw[:]))
                            S.wait("tensor", VM)
                            TE(lambda t: t.matmul(psE[0:2, 0:512], e2_sb[:], sqb[:],
                                                  start=True, stop=True), TM)
                            S.wait("vector", TM)
                            V(lambda v: v.tensor_scalar(l2v[:], psE[0:2, 0:512],
                                                        1e-30, None, ALU.max))
                            S.wait("scalar", VM)
                            SC(lambda s: s.activation(l2v[:], l2v[:], AF.Ln,
                                                      bias=0.0, scale=1.0))
                            SC(lambda s: s.activation(l2v[:], l2v[:], AF.Exp,
                                                      bias=0.0, scale=0.5))
                            S.wait("vector", SM)
                            V(lambda v: v.tensor_scalar(l2v[:], l2v[:], 1e-8,
                                                        None, ALU.max))
                            V(lambda v: v.reciprocal(l2v[:], l2v[:]))
                            if qk == 0:
                                V(lambda v: v.tensor_scalar(
                                    l2v[:], l2v[:], float(1.0 / np.sqrt(128.0)),
                                    None, ALU.mult))
                            S.wait("tensor", VM)
                            TE(lambda t: t.matmul(psE[:, 0:512], e2t_sb[:], l2v[:],
                                                  start=True, stop=True), TM)
                            S.wait("vector", TM)
                            V(lambda v, raw=raw, dst=dst, s4=s4: v.tensor_mul(
                                dst[:, s4 * 512:(s4 + 1) * 512], raw[:],
                                psE[:, 0:512]))
                kc_lo = max(0, qlo_t - (L0_WIN - 1)) if l == 0 else 0

                def spans_for(kc):
                    ql = max(kc, qlo_t) * 128
                    qh = (min(kc + L0_WIN, qhi_t) if l == 0 else qhi_t) * 128
                    return [(sp, min(512, qh - sp)) for sp in range(ql, qh, 512)]

                def bank_subspans(sp, spw):
                    res, a = [], sp
                    base = qlo_t * 128
                    while a < sp + spw:
                        b = min(sp + spw, base + ((a - base) // 512 + 1) * 512)
                        res.append((a, b))
                        a = b
                    return res

                order = []
                for kc in range(kc_lo, qhi_t):
                    for (sp, spw) in spans_for(kc):
                        for which in ("E", "O", "S"):
                            for (a, b) in bank_subspans(sp, spw):
                                bank = (a - qlo_t * 128) // 512
                                key = (which if which != "E" else "C", bank)
                                order.append(key)
                first_ix, last_ix = {}, {}
                for ix, key in enumerate(order):
                    if key not in first_ix:
                        first_ix[key] = ix
                    last_ix[key] = ix
                oix = 0
                for kc in range(kc_lo, qhi_t):
                    for (sp, spw) in spans_for(kc):
                        pbk = psA
                        S.wait("tensor", SM)
                        S.wait("tensor", VM)
                        for p01 in (0, 1):
                            TE(lambda t, kc=kc, p01=p01, sp=sp, spw=spw, pbk=pbk:
                               t.matmul((pbk if p01 == 0 else psB)[:, 0:spw],
                                        kc_sb[p01 * 64:(p01 + 1) * 64,
                                              kc * 128:(kc + 1) * 128],
                                        qc_sb[p01 * 64:(p01 + 1) * 64, sp:sp + spw],
                                        start=True, stop=True,
                                        tile_position=(p01 * 64, 0)),
                               TM if p01 == 1 else None)
                        S.wait("scalar", TM)
                        S.wait("scalar", VM)
                        for qi in range(sp // 128, (sp + spw) // 128):
                            off = qi * 128 - sp
                            qof = qi * 128 - qlo_t * 128
                            for p01 in (0, 1):
                                src_ps = pbk if p01 == 0 else psB
                                dst = pb0 if p01 == 0 else pb1
                                if l == 0:
                                    dlt = qi - kc
                                    SC(lambda s, src_ps=src_ps, dst=dst, off=off,
                                             qof=qof, dlt=dlt:
                                       s.activation(dst[:, qof:qof + 128],
                                                    src_ps[:, off:off + 128], AF.Exp,
                                                    bias=biasv_sb[:, dlt:dlt + 1],
                                                    scale=1.0))
                                else:
                                    SC(lambda s, src_ps=src_ps, dst=dst, off=off, qof=qof:
                                       s.activation(dst[:, qof:qof + 128],
                                                    src_ps[:, off:off + 128], AF.Exp,
                                                    bias=0.0, scale=1.0 / 1.5))
                            if qi == kc:
                                S.wait("vector", SM)
                                V(lambda v, qof=qof: v.tensor_mul(
                                    pb0[:, qof:qof + 128], pb0[:, qof:qof + 128],
                                    tri_sb[:]))
                                V(lambda v, qof=qof: v.tensor_mul(
                                    pb1[:, qof:qof + 128], pb1[:, qof:qof + 128],
                                    tri_sb[:]))
                        if DEBUG_STOP:
                            raise _StopEmit()
                        S.wait("tensor", SM)
                        S.wait("tensor", VM)
                        subs_ab = bank_subspans(sp, spw)
                        newbanks = []
                        _o2 = oix
                        for wi in range(3):
                            for (a, b) in subs_ab:
                                if _o2 == first_ix[order[_o2]]:
                                    bnk = (a - qlo_t * 128) // 512
                                    if (bnk, a - qlo_t * 128, b - qlo_t * 128) not in newbanks:
                                        newbanks.append((bnk, a - qlo_t * 128,
                                                         b - qlo_t * 128))
                                _o2 += 1
                        for (bnk, v0, v1) in newbanks:
                            if v0 > bnk * 512:
                                V(lambda v, bnk=bnk, v0=v0: v.memset(
                                    pb0[:, bnk * 512:v0], 0.0), None)
                                V(lambda v, bnk=bnk, v0=v0: v.memset(
                                    pb1[:, bnk * 512:v0], 0.0), None)
                            if v1 < bnk * 512 + 512:
                                V(lambda v, bnk=bnk, v1=v1: v.memset(
                                    pb0[:, v1:bnk * 512 + 512], 0.0), None)
                                V(lambda v, bnk=bnk, v1=v1: v.memset(
                                    pb1[:, v1:bnk * 512 + 512], 0.0), VM)
                        if newbanks:
                            S.wait("tensor", VM)
                        for wi, which in enumerate(("E", "O", "S")):
                            for ai, (a, b) in enumerate(subs_ab):
                                stf = (oix == first_ix[order[oix]])
                                spf = (oix == last_ix[order[oix]])
                                c0 = a - qlo_t * 128
                                c1 = b - qlo_t * 128
                                if stf:
                                    bnk = c0 // 512
                                    c0 = bnk * 512
                                    c1 = bnk * 512 + 512
                                lastmm = (wi == 2 and ai == len(subs_ab) - 1)
                                if which == "E":
                                    TE(lambda t, kc=kc, c0=c0, c1=c1,
                                             stf=stf, spf=spf, c=c:
                                       t.matmul(psC[0:65, c0:c1],
                                                vprime[:, kc * NH * 65 + (2 * c) * 65:
                                                       kc * NH * 65 + (2 * c) * 65 + 65],
                                                pb0[:, c0:c1], start=stf, stop=spf),
                                       None)
                                elif which == "O":
                                    TE(lambda t, kc=kc, c0=c0, c1=c1,
                                             stf=stf, spf=spf, c=c:
                                       t.matmul(psD[64:128, c0:c1],
                                                vprime[:, kc * NH * 65 + (2 * c + 1) * 65:
                                                       kc * NH * 65 + (2 * c + 1) * 65 + 64],
                                                pb1[:, c0:c1], start=stf, stop=spf,
                                                tile_position=(0, 64)),
                                       None)
                                else:
                                    bnk = c0 // 512
                                    TE(lambda t, c0=c0, c1=c1, bnk=bnk,
                                             stf=stf, spf=spf, psR=psR:
                                       t.matmul((psE if bnk == 0 else psR)
                                                [0:1, c0 - 512 * bnk:c1 - 512 * bnk],
                                                ones128b_sb[:, 0:1],
                                                pb1[:, c0:c1], start=stf, stop=spf),
                                       TM if lastmm else None)
                                oix += 1
                S.wait("vector", TM)
                V(lambda v: v.tensor_copy(rs_sb[64:65, 0:1024], psC[64:65, 0:1024]))
                V(lambda v: v.tensor_copy(rs_sb[0:1, 0:512], psE[0:1, 0:512]))
                V(lambda v, psR=psR: v.tensor_copy(rs_sb[0:1, 512:1024],
                                                   psR[0:1, 0:512]))
                for sp in range(2):
                    S.wait("tensor", VM)
                    TE(lambda t, sp=sp: t.matmul(
                        psA[:, 0:512], onesr_sb[64:65, 0:128],
                        rs_sb[64:65, sp * 512:(sp + 1) * 512],
                        start=True, stop=True, tile_position=(64, 0)), TM)
                    TE(lambda t, sp=sp: t.matmul(
                        psB[:, 0:512], ones1_sb[0:1, 0:128],
                        rs_sb[0:1, sp * 512:(sp + 1) * 512],
                        start=True, stop=True), TM)
                    S.wait("vector", TM)
                    V(lambda v: v.reciprocal(rcp[:, 0:512], psA[:, 0:512]))
                    V(lambda v: v.reciprocal(rcp[:, 512:1024], psB[:, 0:512]))
                    V(lambda v, sp=sp, c=c: v.tensor_mul(
                        ofm[0:64, c * 1024 + sp * 512:c * 1024 + (sp + 1) * 512],
                        psC[0:64, sp * 512:(sp + 1) * 512], rcp[0:64, 0:512]))
                    V(lambda v, sp=sp, c=c: v.tensor_mul(
                        ofm[64:128, c * 1024 + sp * 512:c * 1024 + (sp + 1) * 512],
                        psD[64:128, sp * 512:(sp + 1) * 512], rcp[64:128, 512:1024]))
            S.wait("tensor", VM)
            S.wait("tensor", D_G)
            for tl in range(8):
                i = H * 8 + tl
                for n in range(2):
                    for k in range(DC):
                        TE(lambda t, k=k, n=n, tl=tl: t.matmul(
                            psC[:, n * 512:(n + 1) * 512],
                            ofm[:, k * 1024 + tl * 128:k * 1024 + (tl + 1) * 128],
                            wbig[:, k * D + n * 512:k * D + n * 512 + 512],
                            start=(k == 0), stop=(k == DC - 1)),
                           TM if (k == DC - 1 and n == 1) else None)
                S.wait("vector", TM)
                V(lambda v, i=i: v.tensor_add(xt(i), xt(i), bobc[:]))
                scl = 0.5 if l == 1 else 1.0
                V(lambda v, i=i, scl=scl: v.scalar_tensor_tensor(
                    xt(i), psC[:, 0:1024], scl, xt(i), ALU.mult, ALU.add))
                S.wait("tensor", VM)

    subs = []
    for l in range(2):
        subs += [(l, "Fa"), (l, "At"), (l, "Fb")]
    subs = subs[:n_sub]

    try:
      for seq in range(SEQS):
          if seq > 0:
              S.wait("sync", VM)
              S.dma("sync", y_d[seq - 1].rearrange("(i p) d -> p i d", p=128),
                    x_sb[:].rearrange("p (i d) -> p i d", d=D), ST)
              S.wait("sync", ST)
              S.dma("sync", x_sb[:].rearrange("p (i d) -> p i d", d=D),
                    x_d[seq].rearrange("(i p) d -> p i d", p=128), D_S)
          S.wait("vector", D_S)
          S.wait("scalar", D_S)
          S.wait("tensor", D_S)
          S.wait("vector", D_G)
          for (l, kind) in subs:
              with ExitStack() as pes:
                  if kind == "At":
                      attn_phase(l, seq, pes)
                  else:
                      ffn_phase(l, seq, 0 if kind == "Fa" else 1, pes)
                  S.wait("vector", TM)
                  S.wait("vector", SM)
                  S.wait("tensor", VM)
                  S.wait("tensor", SM)
                  S.wait("scalar", VM)
                  S.wait("scalar", TM)
                  S.wait("gpsimd", VM)
                  S.wait("sync", VM)
      S.wait("sync", VM)
      S.dma("sync", y_d[SEQS - 1].rearrange("(i p) d -> p i d", p=128),
              x_sb[:].rearrange("p (i d) -> p i d", d=D), ST)
      S.wait("sync", ST)
    except _StopEmit:
        pass

    with nc.Block() as block:
        S.build(block)
    return nc, es


def kernel(**inputs):
    host = _prep(inputs)
    x = np.ascontiguousarray(np.asarray(inputs["L0_attn_x"], dtype=np.float32))
    nc, es = build_program(N_SUBLAYERS, DEBUG_TAPS)
    in_maps = []
    for c in range(NCORES := 8):
        m = dict(host)
        m["x"] = np.ascontiguousarray(x[2 * c:2 * c + 2])
        in_maps.append(m)
    res = run_bass_kernel_spmd(nc, in_maps, core_ids=list(range(8)))
    es.close()
    return np.concatenate([np.asarray(res.results[c]["y"]) for c in range(8)],
                          axis=0).astype(np.float32)


# version 03
# speedup vs baseline: 1.4720x; 1.2182x over previous
import numpy as np
import ml_dtypes
from contextlib import ExitStack

import concourse.bass as bass
import concourse.mybir as mybir
from concourse.bass_utils import run_bass_kernel_spmd

F32 = mybir.dt.float32
BF16 = mybir.dt.bfloat16
FP8 = mybir.dt.float8e4
I32 = mybir.dt.int32
AF = mybir.ActivationFunctionType
ALU = mybir.AluOpType
AX = mybir.AxisListType.X

B, T, D, NH, HD, HF = 16, 2048, 1024, 16, 64, 4096
NT, DC, HC = T // 128, D // 128, HF // 128
SEQS = 2
L0_WIN = 4

N_SUBLAYERS = 6
DEBUG_STOP = False
DEBUG_TAPS = False


class _StopEmit(Exception):
    pass


class Sched:
    ENGINES = ("sync", "gpsimd", "tensor", "vector", "scalar")

    def __init__(self, nc, es):
        self.nc = nc
        self.es = es
        self.items = {e: [] for e in self.ENGINES}
        self.sems = {}
        self.cnt = {}

    def sem(self, name):
        if name not in self.sems:
            self.sems[name] = self.es.enter_context(self.nc.semaphore(name))
            self.cnt[name] = 0
        return self.sems[name]

    def emit(self, eng, fn, inc=None, k=1):
        if inc is not None:
            self.sem(inc)
            self.cnt[inc] += k
        self.items[eng].append(("op", fn, inc, k))

    def dma(self, eng, out_ap, in_ap, inc, slow=False):
        self.sem(inc)
        self.cnt[inc] += 16
        self.items[eng].append(("dma", (out_ap, in_ap, slow), inc, 16))

    def wait(self, eng, semname, val=None):
        self.sem(semname)
        if val is None:
            val = self.cnt[semname]
        assert val <= self.cnt[semname]
        if val > 0:
            self.items[eng].append(("wait", semname, val, 0))

    def val(self, semname):
        self.sem(semname)
        return self.cnt[semname]

    def build(self, block):
        def mk(eng_name):
            def run(engine):
                for kind, a, bb, k in self.items[eng_name]:
                    if kind == "wait":
                        engine.wait_ge(self.sems[a], bb)
                    elif kind == "dma":
                        engine.dma_start(out=a[0], in_=a[1],
                                         allow_slow_non_contiguous=a[2]
                                         ).then_inc(self.sems[bb], 16)
                    else:
                        inst = a(engine)
                        if bb is not None:
                            inst.then_inc(self.sems[bb], k)
            return run
        block.sync(mk("sync"))
        block.gpsimd(mk("gpsimd"))
        block.tensor(mk("tensor"))
        block.vector(mk("vector"))
        block.scalar(mk("scalar"))


def _prep(inputs):
    t = lambda a: np.ascontiguousarray(np.asarray(a).T)
    d = {}
    for l, p in ((0, "L0"), (1, "L1")):
        for w, n in (("wqt", "Wq"), ("wkt", "Wk"), ("wvt", "Wv"), ("wot", "Wo")):
            d[f"{w}{l}"] = t(inputs[f"{p}_attn_{n}"])
        for b in ("bq", "bk", "bv", "bo"):
            d[f"{b}{l}"] = np.ascontiguousarray(inputs[f"{p}_attn_{b}"])
    d["w1t0"] = t(inputs["L0_ffn_W1"]); d["w2t0"] = t(inputs["L0_ffn_W2"])
    d["b1_0"] = np.ascontiguousarray(inputs["L0_ffn_b1"])
    d["b2_0"] = np.ascontiguousarray(inputs["L0_ffn_b2"])
    d["w1t1"] = t(inputs["L1_ffn_w1"]); d["w2t1"] = t(inputs["L1_ffn_w2"])
    d["n1w1"] = np.ascontiguousarray(inputs["L1_ffn_norm1_w"])
    d["n2w1"] = np.ascontiguousarray(inputs["L1_ffn_norm2_w"])
    d["asc1"] = np.ascontiguousarray(inputs["L1_ffn_act_scale"]).reshape(1)
    d["abi1"] = np.ascontiguousarray(inputs["L1_ffn_act_bias"]).reshape(1)
    d["ident"] = np.eye(128, dtype=ml_dtypes.bfloat16)
    e2 = np.zeros((128, 2), np.float32)
    e2[0:64, 0] = 1.0
    e2[64:128, 1] = 1.0
    d["e2"] = e2.astype(ml_dtypes.bfloat16)
    d["e2t"] = np.ascontiguousarray(e2.T)
    bj = np.arange(128, dtype=np.float32)
    d["biasv"] = np.ascontiguousarray(
        np.stack([bj - 64.0 - 128.0 * z for z in range(L0_WIN)], axis=1))
    d["tri"] = (np.arange(128)[:, None] <= np.arange(128)[None, :]
                ).astype(ml_dtypes.bfloat16)
    d["epsv"] = np.broadcast_to(np.array([1e-5, 1e-8], np.float32), (128, 2)).copy()
    d["ones1"] = np.ones((1, 128), np.float32)
    d["ones128"] = np.ones((128, 1), np.float32)
    d["onesr"] = np.ones((65, 128), np.float32)
    d["ones128b"] = np.ones((128, 1), ml_dtypes.bfloat16)
    return d


def build_program(n_sub=6, DEBUG_TAPS=False, DEBUG_STOP=False):
    nc = bass.Bass("TRN2", target_bir_lowering=False,
                   detect_race_conditions=False)
    es = ExitStack()
    dram = {}

    def din(name, shape, dtype=F32):
        dram[name] = nc.dram_tensor(name, list(shape), dtype, kind="ExternalInput").ap()

    din("x", [SEQS, T, D])
    for l in range(2):
        for w in ("wqt", "wkt", "wvt", "wot"):
            din(f"{w}{l}", [D, D])
        for b in ("bq", "bk", "bv", "bo"):
            din(f"{b}{l}", [D])
    din("w1t0", [D, HF]); din("w2t0", [HF, D])
    din("b1_0", [HF]); din("b2_0", [D])
    din("w1t1", [D, HF]); din("w2t1", [HF, D])
    din("n1w1", [D]); din("n2w1", [HF])
    din("asc1", [1]); din("abi1", [1])
    din("ident", [128, 128], BF16)
    din("e2", [128, 2], BF16); din("e2t", [2, 128])
    din("biasv", [128, L0_WIN])
    din("tri", [128, 128], BF16)
    din("ones1", [1, 128]); din("ones128", [128, 1]); din("epsv", [128, 2])
    din("onesr", [65, 128]); din("ones128b", [128, 1], BF16)
    x_d = dram["x"]
    y_d = nc.dram_tensor("y", [SEQS, T, D], F32, kind="ExternalOutput").ap()
    taps = {}
    if DEBUG_TAPS:
        for nm, sh, dt in (("t_xq", [2, 128, D], BF16), ("t_h", [2, 128, HF], F32),
                           ("t_hq", [2, 128, HF], BF16), ("t_stt", [2, 128, 16], F32),
                           ("t_w1q", [128, DC * HF], FP8), ("t_w2q", [128, HC * D], FP8),
                           ("t_ws", [128, 4], F32), ("t_m", [2, 128, 1], F32), ("t_b1", [128, HF], F32),
                           ("t_r", [2, 128, 1], F32), ("t_vv", [2, 128, 1], F32)):
            taps[nm] = nc.dram_tensor(nm, sh, dt, kind="ExternalOutput").ap()
    wscr = {}
    for l in range(2):
        wscr[(l, 1)] = nc.dram_tensor(f"w1scr{l}", [D, HF], FP8, kind="Internal").ap()
        wscr[(l, 2)] = nc.dram_tensor(f"w2scr{l}", [HF, D], FP8, kind="Internal").ap()

    sbt = lambda name, shape, dtype=F32: es.enter_context(
        nc.sbuf_tensor(name, list(shape), dtype))
    x_sb = sbt("x_sb", [128, NT * D])
    ident = sbt("ident_sb", [128, 128], BF16)
    e2_sb = sbt("e2_sb", [128, 2], BF16)
    e2t_sb = sbt("e2t_sb", [2, 128])
    biasv_sb = sbt("biasv_sb", [128, L0_WIN])
    tri_sb = sbt("tri_sb", [128, 128], BF16)
    ones1_sb = sbt("ones1_sb", [1, 128])
    eps_sb = sbt("eps_sb", [128, 2])
    ones128_sb = sbt("ones128_sb", [128, 1])
    onesr_sb = sbt("onesr_sb", [65, 128])
    ones128b_sb = sbt("ones128b_sb", [128, 1], BF16)
    ws_sb = sbt("ws_sb", [128, 4])
    ai_sb = sbt("ai_sb", [128, 2])
    st_m = sbt("st_m", [128, NT])
    st_r = sbt("st_r", [128, NT])
    st_r2 = sbt("st_r2", [128, NT])
    st_v = sbt("st_v", [128, NT])
    stt = sbt("stt", [128, 32])
    st2 = sbt("st2", [128, 8])
    spc = sbt("spc", [128, 4])
    _spk = [0]
    nt_a = sbt("nt_a", [128, NT])
    scr = {}
    psA = es.enter_context(nc.psum_tensor("psA", [128, 512], F32))
    psB = es.enter_context(nc.psum_tensor("psB", [128, 512], F32))
    psC = es.enter_context(nc.psum_tensor("psC", [128, 1024], F32))

    S = Sched(nc, es)
    D_S, D_G, TM, VM, SM, ST = "dS", "dG", "tm", "vm", "sm", "st"

    S.dma("sync", x_sb[:].rearrange("p (i d) -> p i d", d=D),
          x_d[0].rearrange("(i p) d -> p i d", p=128), D_S)
    for name, tile in (("ident", ident), ("e2t", e2t_sb), ("biasv", biasv_sb),
                       ("tri", tri_sb), ("ones1", ones1_sb), ("ones128", ones128_sb), ("epsv", eps_sb),
                       ("onesr", onesr_sb), ("ones128b", ones128b_sb)):
        S.dma("sync", tile[:], dram[name][:, :], D_S)
    S.dma("gpsimd", e2_sb[:], dram["e2"][:, :], D_G)
    S.dma("gpsimd", ai_sb[:, 0:1], dram["asc1"][None, :].broadcast_to((128, 1)), D_G)
    S.dma("gpsimd", ai_sb[:, 1:2], dram["abi1"][None, :].broadcast_to((128, 1)), D_G)
    S.emit("vector", lambda v: v.memset(spc[:], 0.0), "vm")
    S.emit("vector", lambda v: v.memset(nt_a[:], 0.0), "vm")

    def V(fn, inc=VM):
        S.emit("vector", fn, inc)

    def _spacer(inc):
        k = _spk[0] % 4
        _spk[0] += 1
        S.emit("vector", lambda v, k=k: v.tensor_copy(spc[:, k:k + 1],
                                                      eps_sb[:, 0:1]), inc)

    def Vt(fn):
        S.emit("vector", fn, None)
        _spacer(VM)

    def Vred(fn):
        S.emit("vector", fn, VM)
        _spacer(None)
        _spacer(VM)
        S.wait("vector", VM)

    def SC(fn, inc=SM):
        S.emit("scalar", fn, inc)

    def TE(fn, inc=None):
        S.emit("tensor", fn, inc)

    def xt(i):
        return x_sb[:, i * D:(i + 1) * D]

    def ternarize(tes, wt_d, R, C, out_sb, ws_col, stash_d):
        nsl = R // 128
        sc = lambda n, sh, dt=F32: tes.enter_context(nc.sbuf_tensor(n, list(sh), dt))
        psW = tes.enter_context(nc.psum_tensor(f"psW{ws_col}", [128, 512], F32))
        stg = [sc(f"tg{ws_col}a", [128, C]), sc(f"tg{ws_col}b", [128, C])]
        pacc = sc(f"pc{ws_col}", [128, nsl])
        ab = sc(f"ta{ws_col}", [128, 2048])
        chans = ("twA", "twB")
        for s in range(nsl):
            buf = stg[s % 2]
            ch = chans[s % 2]
            S.wait("gpsimd", VM)
            S.dma("gpsimd", buf[:], wt_d[s * 128:(s + 1) * 128, :], ch)
            S.wait("vector", ch)
            Vred(lambda v, b=buf, s=s: v.tensor_reduce(
                pacc[:, s:s + 1], b[:], AX, ALU.add, apply_absolute_value=True))
        Vred(lambda v, nsl=nsl: v.tensor_reduce(stt[:, 0:1], pacc[:, 0:nsl], AX, ALU.add))
        _spacer(VM)
        S.wait("tensor", VM)
        TE(lambda t: t.matmul(psW[0:1, 0:1], ones128_sb[:], stt[:, 0:1],
                              start=True, stop=True), TM)
        S.wait("vector", TM)
        Vt(lambda v: v.tensor_copy(stt[0:1, 1:2], psW[0:1, 0:1]))
        _spacer(VM)
        S.wait("tensor", VM)
        TE(lambda t: t.matmul(psW[:, 4:5], ones1_sb[:], stt[0:1, 1:2],
                              start=True, stop=True), TM)
        S.wait("vector", TM)
        mmul = 1.0 / float(R * C)
        Vt(lambda v: v.tensor_scalar(stt[:, 2:3], psW[:, 4:5], mmul, None, ALU.mult))
        Vt(lambda v, w=ws_col: v.tensor_scalar(ws_sb[:, w:w + 1], stt[:, 2:3],
                                               1e-5, None, ALU.max))
        Vt(lambda v, w=ws_col: v.tensor_scalar(stt[:, 3:4], ws_sb[:, w:w + 1],
                                               0.5, None, ALU.mult))
        Vt(lambda v, w=ws_col: v.tensor_scalar(stt[:, 4:5], ws_sb[:, w:w + 1],
                                               -0.5, None, ALU.mult))
        for s in range(nsl):
            buf = stg[s % 2]
            ch = chans[s % 2]
            S.wait("gpsimd", VM)
            S.dma("gpsimd", buf[:], wt_d[s * 128:(s + 1) * 128, :], ch)
            S.wait("vector", ch)
            for j in range(0, C, 1024):
                w = min(1024, C - j)
                V(lambda v, b=buf, j=j, w=w: v.tensor_scalar(
                    ab[:, 0:w], b[:, j:j + w], stt[:, 3:4], None, ALU.is_ge), None)
                V(lambda v, b=buf, j=j, w=w: v.tensor_scalar(
                    ab[:, 1024:1024 + w], b[:, j:j + w], stt[:, 4:5], None,
                    ALU.is_le), None)
                V(lambda v, s=s, j=j, w=w: v.tensor_sub(
                    out_sb[:, s * C + j:s * C + j + w], ab[:, 0:w],
                    ab[:, 1024:1024 + w]))
        S.wait("gpsimd", VM)
        S.dma("gpsimd", stash_d.rearrange("(s p) c -> p s c", p=128),
              out_sb[:, 0:nsl * C].rearrange("p (s c) -> p s c", c=C), "twS")

    def load_fp8(stash_d, out_sb, nsl, C):
        S.wait("gpsimd", "twS")
        S.dma("gpsimd", out_sb[:, 0:nsl * C].rearrange("p (s c) -> p s c", c=C),
              stash_d.rearrange("(s p) c -> p s c", p=128), D_G)

    def ln_prepass(want_r2, junk):
        for i in range(NT):
            Vred(lambda v, i=i: v.reduce_sum(stt[:, 0:1], xt(i), AX))
            V(lambda v, i=i: v.tensor_scalar(st_m[:, i:i + 1], stt[:, 0:1],
                                             1.0 / D, None, ALU.mult))
            S.wait("scalar", VM)
            SC(lambda s, i=i, junk=junk: s.activation(
                junk[:, 0:D], xt(i), AF.Square, accum_out=stt[:, 1:2]))
            S.wait("vector", SM)
            Vt(lambda v: v.tensor_scalar(stt[:, 2:3], stt[:, 1:2], 1.0 / D,
                                         None, ALU.mult))
            Vt(lambda v, i=i: v.tensor_mul(stt[:, 3:4], st_m[:, i:i + 1],
                                           st_m[:, i:i + 1]))
            Vt(lambda v, i=i: v.tensor_sub(st_v[:, i:i + 1], stt[:, 2:3],
                                           stt[:, 3:4]))
        Vt(lambda v: v.tensor_scalar(st_v[:], st_v[:], 1e-5, None, ALU.add))
        S.wait("scalar", VM)
        SC(lambda s: s.activation(st_r[:], st_v[:], AF.Ln, bias=0.0, scale=1.0))
        SC(lambda s: s.activation(st_r[:], st_r[:], AF.Exp, bias=0.0, scale=-0.5))
        S.wait("vector", SM)
        newton_rsqrt(st_r[:], st_v[:], NT)
        if want_r2:
            Vt(lambda v: v.tensor_mul(st_r2[:], st_r[:], st_r[:]))
            Vt(lambda v: v.tensor_scalar(st_v[:], st_v[:], -1e-5, None, ALU.add))
            Vt(lambda v: v.tensor_mul(st_r2[:], st_r2[:], st_v[:]))
            Vt(lambda v: v.tensor_scalar(st_r2[:], st_r2[:], 1e-8, None, ALU.add))
            Vt(lambda v: v.tensor_copy(st_v[:], st_r2[:]))
            S.wait("scalar", VM)
            SC(lambda s: s.activation(st_r2[:], st_r2[:], AF.Ln, bias=0.0, scale=1.0))
            SC(lambda s: s.activation(st_r2[:], st_r2[:], AF.Exp, bias=0.0, scale=-0.5))
            S.wait("vector", SM)
            newton_rsqrt(st_r2[:], st_v[:], NT)
        S.wait("vector", SM)

    def newton_rsqrt(rt, vt2, W):
        Vt(lambda v: v.tensor_mul(nt_a[:, 0:W], rt, rt))
        Vt(lambda v: v.tensor_mul(nt_a[:, 0:W], nt_a[:, 0:W], vt2))
        Vt(lambda v: v.tensor_scalar(nt_a[:, 0:W], nt_a[:, 0:W], -0.5, 1.5,
                                     ALU.mult, ALU.add))
        Vt(lambda v: v.tensor_mul(rt, rt, nt_a[:, 0:W]))

    def ffn_phase(l, seq, half, pes):
        sc = lambda n, sh, dt=F32: pes.enter_context(nc.sbuf_tensor(n, list(sh), dt))
        tag = f"f{l}{seq}{half}"
        w1q = sc(f"w1q{tag}", [128, DC * HF], FP8)
        w2q = sc(f"w2q{tag}", [128, HC * D], FP8)

        if seq == 0 and half == 0:
            with ExitStack() as tes:
                ternarize(tes, dram[f"w1t{l}"], D, HF, w1q, 2 * l, wscr[(l, 1)])
                ternarize(tes, dram[f"w2t{l}"], HF, D, w2q, 2 * l + 1, wscr[(l, 2)])
        else:
            load_fp8(wscr[(l, 1)], w1q, DC, HF)
            load_fp8(wscr[(l, 2)], w2q, HC, D)

        psTx = pes.enter_context(nc.psum_tensor(f"psTx{tag}", [128, 1024], BF16))
        psTh = [pes.enter_context(nc.psum_tensor(f"psTh{t}{tag}", [128, 1024],
                                                 BF16)) for t in range(2)]
        tbuf = [sc(f"tb{p}{tag}", [128, D]) for p in range(2)]
        xq_tm = [sc(f"xq{p}{tag}", [128, D], BF16) for p in range(2)]
        xqT = sc(f"xqT{tag}", [128, D], BF16)
        qx = sc(f"qx{tag}", [128, 1024], I32)
        qh = sc(f"qh{tag}", [128, 1024], I32)
        debf = sc(f"debf{tag}", [128, 1024])
        h_sb = sc(f"h{tag}", [128, HF])
        hq_tm = sc(f"hq{tag}", [128, HF], BF16)
        hqT = sc(f"hqT{tag}", [128, 2048], BF16)
        if l == 0:
            b1bc = sc(f"b1bc{tag}", [128, HF])
            b2bc = sc(f"b2bc{tag}", [128, D])
            S.wait("sync", VM)
            S.dma("sync", b1bc[:], dram["b1_0"][None, :].broadcast_to((128, HF)), D_S)
            S.dma("sync", b2bc[:], dram["b2_0"][None, :].broadcast_to((128, D)), D_S)
        else:
            n1bc = sc(f"n1bc{tag}", [128, D])
            n2bc = sc(f"n2bc{tag}", [128, HF])
            S.wait("sync", VM)
            S.dma("sync", n1bc[:], dram["n1w1"][None, :].broadcast_to((128, D)), D_S)
            S.dma("sync", n2bc[:], dram["n2w1"][None, :].broadcast_to((128, HF)), D_S)

        ln_prepass(want_r2=(l == 1), junk=xq_tm[0])
        S.wait("vector", D_S)
        S.wait("scalar", D_S)
        S.wait("tensor", D_G)
        S.wait("vector", D_G)

        vm_xq = [0] * NT; vm_s10 = [0] * NT; vm_scq = [0] * NT
        tm_t1 = [0] * NT; sm_c1 = [0] * NT
        tm_f1 = [[0] * 8 for _ in range(NT)]
        bankfree = [[0] * 8 for _ in range(NT)]
        bf_sem = VM if l == 0 else SM
        sm_act7 = [0] * NT
        vm_hqs = [[0] * 8 for _ in range(NT)]
        tm_t2r = [[0] * 4 for _ in range(NT)]
        sm_c2 = [[0] * 4 for _ in range(NT)]
        tm_f2g = [[0] * 4 for _ in range(NT)]
        vm_res = [0] * NT

        def chain3(src, s_ap, qreg, dst):
            V(lambda v: v.scalar_tensor_tensor(debf[:, 0:512], src, s_ap, qreg,
                                               ALU.mult, ALU.subtract))
            V(lambda v: v.scalar_tensor_tensor(debf[:, 512:1024], debf[:, 0:512],
                                               1.5, qreg, ALU.is_gt, ALU.add))
            V(lambda v: v.scalar_tensor_tensor(dst, debf[:, 0:512], 0.5,
                                               debf[:, 512:1024], ALU.is_ge,
                                               ALU.add))

        def emit_P(i):
            p = i % 2; po = 16 * p
            m_ap = st_m[:, i:i + 1]
            if l == 0:
                V(lambda v, i=i, m=m_ap, p=p: v.tensor_scalar(
                    tbuf[p][:, 0:D], xt(i), m, None, ALU.subtract))
            else:
                V(lambda v, i=i, m=m_ap, p=p: v.scalar_tensor_tensor(
                    tbuf[p][:, 0:D], xt(i), m, n1bc[:], ALU.subtract, ALU.mult))
            Vred(lambda v, p=p, po=po: v.tensor_reduce(
                stt[:, po + 5:po + 6], tbuf[p][:, 0:D], AX, ALU.max,
                apply_absolute_value=True))
            if l == 0:
                Vt(lambda v, i=i, po=po: v.tensor_copy(stt[:, po + 9:po + 10],
                                                       st_r[:, i:i + 1]))
            else:
                Vt(lambda v, i=i, po=po: v.tensor_mul(
                    stt[:, po + 9:po + 10], st_r[:, i:i + 1], st_r2[:, i:i + 1]))
            Vt(lambda v, po=po: v.tensor_mul(stt[:, po + 6:po + 7],
                                             stt[:, po + 5:po + 6],
                                             stt[:, po + 9:po + 10]))
            Vt(lambda v, po=po: v.tensor_scalar(stt[:, po + 6:po + 7],
                                                stt[:, po + 6:po + 7], 1e-5,
                                                None, ALU.max))
            Vt(lambda v, po=po: v.reciprocal(stt[:, po + 7:po + 8],
                                             stt[:, po + 6:po + 7]))
            Vt(lambda v, po=po: v.tensor_scalar(stt[:, po + 7:po + 8],
                                                stt[:, po + 7:po + 8], 127.0,
                                                None, ALU.mult))
            Vt(lambda v, po=po: v.tensor_mul(stt[:, po + 8:po + 9],
                                             stt[:, po + 7:po + 8],
                                             stt[:, po + 9:po + 10]))
            Vt(lambda v, po=po: v.tensor_scalar(stt[:, po + 10:po + 11],
                                                stt[:, po + 6:po + 7],
                                                1.0 / 127.0, None, ALU.mult))
            Vt(lambda v, po=po, l=l: v.tensor_mul(stt[:, po + 10:po + 11],
                                                  stt[:, po + 10:po + 11],
                                                  ws_sb[:, 2 * l:2 * l + 1]))
            _spacer(VM)
            vm_s10[i] = S.val(VM)
            for sl in range(2):
                qreg = qx[:, sl * 512:(sl + 1) * 512]
                if sl == 0:
                    S.wait("scalar", VM, vm_s10[i])
                SC(lambda s, p=p, po=po, sl=sl, qreg=qreg: s.activation(
                    qreg, tbuf[p][:, sl * 512:(sl + 1) * 512], AF.Copy,
                    bias=-1.0, scale=stt[:, po + 8:po + 9]))
                smv = S.val(SM)
                S.wait("vector", SM, smv)
                chain3(tbuf[p][:, sl * 512:(sl + 1) * 512],
                       stt[:, po + 8:po + 9], qreg,
                       xq_tm[p][:, sl * 512:(sl + 1) * 512])
            vm_xq[i] = S.val(VM)

        def emit_T1(i):
            p = i % 2
            if i > 0:
                S.wait("tensor", SM, sm_c1[i - 1])
            S.wait("tensor", VM, vm_xq[i])
            for k in range(DC):
                TE(lambda t, k=k, p=p: t.transpose(
                    psTx[:, k * 128:(k + 1) * 128],
                    xq_tm[p][:, k * 128:(k + 1) * 128], ident[:]),
                   TM if k == DC - 1 else None)
            tm_t1[i] = S.val(TM)

        def emit_C1(i):
            wv = tm_t1[i] if i == 0 else max(tm_t1[i], tm_f1[i - 1][7])
            S.wait("scalar", TM, wv)
            SC(lambda s: s.activation(xqT[:, 0:D], psTx[:, 0:D], AF.Copy,
                                      bias=0.0, scale=1.0))
            sm_c1[i] = S.val(SM)

        def emit_F1(i):
            S.wait("tensor", SM, sm_c1[i])
            for n in range(8):
                pb = psA if n % 2 == 0 else psB
                if i > 0:
                    S.wait("tensor", bf_sem, bankfree[i - 1][n])
                for k in range(DC):
                    TE(lambda t, k=k, n=n, pb=pb: t.matmul(
                        pb[:, 0:512], xqT[:, k * 128:(k + 1) * 128],
                        w1q[:, k * HF + n * 512:k * HF + n * 512 + 512],
                        start=(k == 0), stop=(k == DC - 1)),
                       TM if k == DC - 1 else None)
                tm_f1[i][n] = S.val(TM)

        def emit_E1(i):
            p = i % 2; po = 16 * p
            if l == 0:
                for n in range(8):
                    pb = psA if n % 2 == 0 else psB
                    S.wait("vector", TM, tm_f1[i][n])
                    V(lambda v, n=n, pb=pb, po=po: v.scalar_tensor_tensor(
                        h_sb[:, n * 512:(n + 1) * 512], pb[:, 0:512],
                        stt[:, po + 10:po + 11], b1bc[:, n * 512:(n + 1) * 512],
                        ALU.mult, ALU.add))
                    bankfree[i][n] = S.val(VM)
                    S.wait("scalar", VM, bankfree[i][n])
                    SC(lambda s, n=n: s.activation(
                        h_sb[:, n * 512:(n + 1) * 512],
                        h_sb[:, n * 512:(n + 1) * 512], AF.Gelu))
                sm_act7[i] = S.val(SM)
            else:
                S.wait("scalar", VM, vm_s10[i])
                for n in range(8):
                    pb = psA if n % 2 == 0 else psB
                    S.wait("scalar", TM, tm_f1[i][n])
                    SC(lambda s, n=n, pb=pb, po=po: s.activation(
                        h_sb[:, n * 512:(n + 1) * 512], pb[:, 0:512], AF.Relu,
                        bias=0.0, scale=stt[:, po + 10:po + 11]))
                    bankfree[i][n] = S.val(SM)
                    SC(lambda s, n=n: s.activation(
                        h_sb[:, n * 512:(n + 1) * 512],
                        h_sb[:, n * 512:(n + 1) * 512], AF.Square))
                    smv = S.val(SM)
                    S.wait("vector", SM, smv)
                    V(lambda v, n=n: v.tensor_scalar(
                        h_sb[:, n * 512:(n + 1) * 512],
                        h_sb[:, n * 512:(n + 1) * 512],
                        ai_sb[:, 0:1], ai_sb[:, 1:2], ALU.mult, ALU.add))

        def emit_rms2(i):
            p = i % 2; po = 16 * p
            vm_ts = S.val(VM)
            S.wait("scalar", VM, vm_ts)
            for j4 in range(4):
                SC(lambda s, j4=j4, p=p: s.activation(
                    xq_tm[1 - p][:, 0:1024], h_sb[:, j4 * 1024:(j4 + 1) * 1024],
                    AF.Square, accum_out=st2[:, 4 * p + j4:4 * p + j4 + 1]))
            smv = S.val(SM)
            S.wait("vector", SM, smv)
            Vred(lambda v, p=p, po=po: v.tensor_reduce(
                stt[:, po + 11:po + 12], st2[:, 4 * p:4 * p + 4], AX, ALU.add))
            Vt(lambda v, po=po: v.tensor_scalar(
                stt[:, po + 11:po + 12], stt[:, po + 11:po + 12],
                1.0 / HF, 1e-8, ALU.mult, ALU.add))
            Vt(lambda v, po=po: v.tensor_copy(stt[:, po + 15:po + 16],
                                              stt[:, po + 11:po + 12]))
            _spacer(VM)
            vmv = S.val(VM)
            S.wait("scalar", VM, vmv)
            SC(lambda s, po=po: s.activation(stt[:, po + 11:po + 12],
                                             stt[:, po + 11:po + 12], AF.Ln,
                                             bias=0.0, scale=1.0))
            SC(lambda s, po=po: s.activation(stt[:, po + 11:po + 12],
                                             stt[:, po + 11:po + 12], AF.Exp,
                                             bias=0.0, scale=-0.5))
            smv = S.val(SM)
            S.wait("vector", SM, smv)
            newton_rsqrt(stt[:, po + 11:po + 12], stt[:, po + 15:po + 16], 1)
            V(lambda v, po=po: v.scalar_tensor_tensor(
                h_sb[:, 0:HF], h_sb[:, 0:HF], stt[:, po + 11:po + 12], n2bc[:],
                ALU.mult, ALU.mult))

        def emit_Q2(i):
            p = i % 2; po = 16 * p
            if l == 0:
                S.wait("vector", SM, sm_act7[i])
            Vred(lambda v, po=po: v.tensor_reduce(
                stt[:, po + 12:po + 13], h_sb[:, 0:HF], AX, ALU.max,
                apply_absolute_value=True))
            Vt(lambda v, po=po: v.tensor_scalar(stt[:, po + 12:po + 13],
                                                stt[:, po + 12:po + 13], 1e-5,
                                                None, ALU.max))
            Vt(lambda v, po=po: v.reciprocal(stt[:, po + 13:po + 14],
                                             stt[:, po + 12:po + 13]))
            Vt(lambda v, po=po: v.tensor_scalar(stt[:, po + 13:po + 14],
                                                stt[:, po + 13:po + 14], 127.0,
                                                None, ALU.mult))
            Vt(lambda v, po=po: v.tensor_scalar(stt[:, po + 14:po + 15],
                                                stt[:, po + 12:po + 13],
                                                0.5 / 127.0, None, ALU.mult))
            Vt(lambda v, po=po, l=l: v.tensor_mul(stt[:, po + 14:po + 15],
                                                  stt[:, po + 14:po + 15],
                                                  ws_sb[:, 2 * l + 1:2 * l + 2]))
            _spacer(VM)
            vm_scq[i] = S.val(VM)
            for sl in range(8):
                qreg = qh[:, (sl % 2) * 512:(sl % 2) * 512 + 512]
                wv = vm_scq[i] if sl < 2 else vm_hqs[i][sl - 2]
                S.wait("scalar", VM, wv)
                SC(lambda s, sl=sl, po=po, qreg=qreg: s.activation(
                    qreg, h_sb[:, sl * 512:(sl + 1) * 512], AF.Copy, bias=-1.0,
                    scale=stt[:, po + 13:po + 14]))
                smv = S.val(SM)
                S.wait("vector", SM, smv)
                if sl == 0 and i > 0:
                    S.wait("vector", TM, tm_t2r[i - 1][3])
                chain3(h_sb[:, sl * 512:(sl + 1) * 512],
                       stt[:, po + 13:po + 14], qreg,
                       hq_tm[:, sl * 512:(sl + 1) * 512])
                vm_hqs[i][sl] = S.val(VM)

        def _t2round(j, rr):
            S.wait("tensor", VM, vm_hqs[j][2 * rr + 1])
            if rr >= 2:
                S.wait("tensor", SM, sm_c2[j][rr - 2])
            for k in range(8):
                TE(lambda t, k=k, rr=rr: t.transpose(
                    psTh[rr % 2][:, k * 128:(k + 1) * 128],
                    hq_tm[:, rr * 1024 + k * 128:rr * 1024 + (k + 1) * 128],
                    ident[:]), TM if k == 7 else None)
            tm_t2r[j][rr] = S.val(TM)

        def _c2(j, rr):
            wv = tm_t2r[j][rr]
            if rr >= 2:
                wv = max(wv, tm_f2g[j][rr - 2])
            elif j > 0:
                wv = max(wv, tm_f2g[j - 1][2 + rr])
            S.wait("scalar", TM, wv)
            SC(lambda s, rr=rr: s.activation(
                hqT[:, (rr % 2) * 1024:(rr % 2) * 1024 + 1024],
                psTh[rr % 2][:, 0:1024], AF.Copy, bias=0.0, scale=1.0))
            sm_c2[j][rr] = S.val(SM)

        def _f2group(j, g):
            S.wait("tensor", SM, sm_c2[j][g])
            if g == 0 and j > 0:
                S.wait("tensor", VM, vm_res[j - 1])
            for k in range(8):
                kk = g * 8 + k
                for n in range(2):
                    TE(lambda t, k=k, kk=kk, n=n, g=g: t.matmul(
                        psC[:, n * 512:(n + 1) * 512],
                        hqT[:, (g % 2) * 1024 + k * 128:
                            (g % 2) * 1024 + (k + 1) * 128],
                        w2q[:, kk * D + n * 512:kk * D + n * 512 + 512],
                        start=(kk == 0), stop=(kk == HC - 1)),
                       TM if (k == 7 and n == 1) else None)
            tm_f2g[j][g] = S.val(TM)

        def emit_T2F2C2(j):
            _t2round(j, 0)
            _c2(j, 0)
            _t2round(j, 1)
            _c2(j, 1)
            _f2group(j, 0)
            _t2round(j, 2)
            _c2(j, 2)
            _f2group(j, 1)
            _t2round(j, 3)
            _c2(j, 3)
            _f2group(j, 2)
            _f2group(j, 3)

        def emit_R(j):
            po = 16 * (j % 2)
            S.wait("vector", TM, tm_f2g[j][3])
            if l == 0:
                V(lambda v, j=j: v.scalar_tensor_tensor(
                    xt(j), b2bc[:], 0.5, xt(j), ALU.mult, ALU.add))
            V(lambda v, j=j, po=po: v.scalar_tensor_tensor(
                xt(j), psC[:, 0:1024], stt[:, po + 14:po + 15], xt(j),
                ALU.mult, ALU.add))
            vm_res[j] = S.val(VM)

        emit_P(0)
        for i in range(NT + 1):
            if i < NT:
                emit_T1(i)
                emit_C1(i)
                emit_F1(i)
                emit_E1(i)
            if i >= 1:
                emit_T2F2C2(i - 1)
            if i < NT:
                if l == 1:
                    emit_rms2(i)
                emit_Q2(i)
            if i >= 1:
                emit_R(i - 1)
            if i + 1 < NT:
                emit_P(i + 1)

    def attn_phase(l, seq, pes):
        sc = lambda n, sh, dt=F32: pes.enter_context(nc.sbuf_tensor(n, list(sh), dt))
        tag = f"a{l}{seq}"
        xhT = sc(f"xhT{tag}", [128, DC * T], BF16)
        vprime = sc(f"vp{tag}", [128, NT * NH * 65], BF16)
        wbig = sc(f"wbig{tag}", [128, DC * D], BF16)
        ofm = sc(f"ofm{tag}", [128, DC * 1024], BF16)
        qc_sb = sc(f"qc{tag}", [128, T], BF16)
        kc_sb = sc(f"kc{tag}", [128, T], BF16)
        wqk = [sc(f"wqk{t}{tag}", [128, DC * 256], BF16) for t in range(2)]
        pb0 = sc(f"pb0{tag}", [128, 1024], BF16)
        pb1 = sc(f"pb1{tag}", [128, 1024], BF16)
        rs_sb = sc(f"rs{tag}", [65, 1024])
        rcp = sc(f"rcp{tag}", [128, 512])
        batt = sc(f"batt{tag}", [128, 4 * DC])
        bvbc = sc(f"bvbc{tag}", [128, D])
        bobc = sc(f"bobc{tag}", [128, D])
        rb = [sc(f"ra{tag}", [128, 512]), sc(f"rbb{tag}", [128, 512])]
        sqb = sc(f"sqb{tag}", [128, 512], BF16)
        l2v = sc(f"l2v{tag}", [2, 1024])
        xh_tm = qc_sb[:, 0:D]
        SCALE = float(1.0 / np.sqrt(128.0))

        for w, b in ((0, "bq"), (1, "bk"), (2, "bv"), (3, "bo")):
            S.dma("sync", batt[:, w * DC:(w + 1) * DC],
                  dram[f"{b}{l}"].rearrange("(c p) -> p c", p=128), D_S, slow=True)
        S.dma("sync", bvbc[:], dram[f"bv{l}"][None, :].broadcast_to((128, D)), D_S)
        S.dma("sync", bobc[:], dram[f"bo{l}"][None, :].broadcast_to((128, D)), D_S)
        S.dma("gpsimd", wbig[:].rearrange("p (k o) -> p k o", o=D),
              dram[f"wvt{l}"].rearrange("(k p) o -> p k o", p=128), D_G)

        ln_prepass(want_r2=False, junk=xh_tm)
        S.wait("vector", D_S)
        if l == 1:
            V(lambda v: v.tensor_scalar(bobc[:], bobc[:], 0.5, None, ALU.mult))

        with ExitStack() as tscope:
            psT = [tscope.enter_context(nc.psum_tensor(f"psT{t}{tag}",
                                                       [128, 1024], BF16))
                   for t in range(2)]
            tm_tr = [0] * NT
            vm_cp = [0] * NT
            for i in range(NT):
                V(lambda v, i=i: v.tensor_scalar(
                    xh_tm[:], xt(i), st_m[:, i:i + 1], st_r[:, i:i + 1],
                    ALU.subtract, ALU.mult))
                vmn = S.val(VM)
                S.wait("tensor", VM, vmn)
                if i >= 2:
                    S.wait("tensor", VM, vm_cp[i - 2])
                for k in range(DC):
                    TE(lambda t, k=k, i=i: t.transpose(
                        psT[i % 2][:, k * 128:(k + 1) * 128],
                        xh_tm[:, k * 128:(k + 1) * 128],
                        ident[:]), TM if k == DC - 1 else None)
                tm_tr[i] = S.val(TM)
                S.wait("vector", TM, tm_tr[i])
                V(lambda v, i=i: v.tensor_copy(
                    xhT[:].rearrange("p (k t) -> p k t", t=T)
                    [:, :, i * 128:(i + 1) * 128],
                    psT[i % 2][:, 0:D].rearrange("p (k c) -> p k c", c=128)))
                vm_cp[i] = S.val(VM)
        psD = pes.enter_context(nc.psum_tensor(f"psD{tag}", [128, 1024], F32))
        psE = pes.enter_context(nc.psum_tensor(f"psE{tag}", [128, 512], F32))
        psR = pes.enter_context(nc.psum_tensor(f"psR{tag}", [128, 512], F32))

        S.wait("tensor", D_G)
        vm_vp = [0] * NT
        tm_vp = [0] * NT
        for i in range(NT):
            pp = psC if i % 2 == 0 else psD
            if i >= 2:
                S.wait("tensor", VM, vm_vp[i - 2])
            for n in range(2):
                for k in range(DC):
                    TE(lambda t, i=i, n=n, k=k, pp=pp: t.matmul(
                        pp[:, n * 512:(n + 1) * 512],
                        xhT[:, k * T + i * 128:k * T + (i + 1) * 128],
                        wbig[:, k * D + n * 512:k * D + n * 512 + 512],
                        start=(k == 0), stop=(k == DC - 1)),
                       TM if k == DC - 1 else None)
            tm_vp[i] = S.val(TM)
            S.wait("vector", TM, tm_vp[i])
            for n in range(2):
                V(lambda v, i=i, n=n, pp=pp: v.tensor_tensor(
                    vprime[:, i * NH * 65:(i + 1) * NH * 65]
                    .rearrange("p (h c) -> p h c", c=65)[:, 8 * n:8 * n + 8, 0:64],
                    pp[:, n * 512:(n + 1) * 512]
                    .rearrange("p (h c) -> p h c", c=64),
                    bvbc[:, n * 512:(n + 1) * 512]
                    .rearrange("p (h c) -> p h c", c=64), ALU.add))
            V(lambda v, i=i: v.memset(
                vprime[:, i * NH * 65:(i + 1) * NH * 65]
                .rearrange("p (h c) -> p h c", c=65)[:, :, 64:65], 1.0))
            vm_vp[i] = S.val(VM)
        S.wait("gpsimd", TM, tm_vp[NT - 1])
        S.dma("gpsimd", wbig[:].rearrange("p (k o) -> p k o", o=D),
              dram[f"wot{l}"].rearrange("(k p) o -> p k o", p=128), D_G)
        dg_wot = S.val(D_G)

        pv_free_vm = vm_vp[NT - 1]
        sepr_free_sm = 0
        sm_ab_free = 0
        rs_free_tm = 0
        last_pv_tm = [0, 0]
        qk_read_tm = [0, 0]
        dg_wqk = [0, 0]
        for H in range(2):
            qlo_t, qhi_t = H * 8, H * 8 + 8
            for c in range(8):
                par = (H * 8 + c) % 2
                S.wait("gpsimd", TM, qk_read_tm[par])
                S.dma("gpsimd",
                      wqk[par][:, 0:DC * 128].rearrange("p (k o) -> p k o", o=128),
                      dram[f"wqt{l}"][:, c * 128:(c + 1) * 128]
                      .rearrange("(k p) o -> p k o", p=128), D_G)
                S.dma("gpsimd",
                      wqk[par][:, DC * 128:].rearrange("p (k o) -> p k o", o=128),
                      dram[f"wkt{l}"][:, c * 128:(c + 1) * 128]
                      .rearrange("(k p) o -> p k o", p=128), D_G)
                dg_wqk[par] = S.val(D_G)

                chunks = ([(0, s4) for s4 in range(2 * H, 2 * H + 2)] +
                          [(1, s4) for s4 in range(0, 2 * (H + 1))])
                nch = len(chunks)
                tm_proj = [0] * nch
                vm_bf = [0] * nch
                vm_sqb = [0] * nch
                tm_e2 = [0] * nch
                tm_e2t = [0] * nch
                vm_l2 = [0] * nch
                vm_mult = [0] * nch

                def bank_of(ti):
                    return psA if ti % 2 == 0 else psB

                def emit_proj(ti):
                    qk, s4 = chunks[ti]
                    pbk = bank_of(ti)
                    wofs = qk * DC * 128
                    if ti == 0:
                        S.wait("tensor", D_G, dg_wqk[par])
                        if sm_ab_free:
                            S.wait("tensor", SM, sm_ab_free)
                    if ti >= 2:
                        S.wait("tensor", VM, vm_bf[ti - 2])
                    wq = wqk[par]
                    for k in range(DC):
                        TE(lambda t, k=k, s4=s4, pbk=pbk, wofs=wofs, wq=wq:
                           t.matmul(
                            pbk[:, 0:512],
                            wq[:, wofs + k * 128:wofs + (k + 1) * 128],
                            xhT[:, k * T + s4 * 512:k * T + (s4 + 1) * 512],
                            start=(k == 0), stop=(k == DC - 1)),
                           TM if k == DC - 1 else None)
                    tm_proj[ti] = S.val(TM)

                def emit_evac_l0(ti):
                    qk, s4 = chunks[ti]
                    pbk = bank_of(ti)
                    dst = qc_sb if qk == 0 else kc_sb
                    S.wait("vector", TM, tm_proj[ti])
                    if qk == 0:
                        V(lambda v, s4=s4, pbk=pbk, dst=dst, c=c: v.tensor_scalar(
                            dst[:, s4 * 512:(s4 + 1) * 512], pbk[:, 0:512],
                            batt[:, c:c + 1], 0.125, ALU.add, ALU.mult))
                    else:
                        V(lambda v, s4=s4, pbk=pbk, dst=dst, c=c: v.tensor_scalar(
                            dst[:, s4 * 512:(s4 + 1) * 512], pbk[:, 0:512],
                            batt[:, DC + c:DC + c + 1], None, ALU.add))
                    vm_bf[ti] = S.val(VM)

                def emit_raw(ti):
                    qk, s4 = chunks[ti]
                    pbk = bank_of(ti)
                    bcol = (0 if qk == 0 else DC) + c
                    S.wait("vector", TM, tm_proj[ti])
                    V(lambda v, ti=ti, pbk=pbk, bcol=bcol: v.tensor_scalar(
                        rb[ti % 2][:], pbk[:, 0:512], batt[:, bcol:bcol + 1],
                        None, ALU.add))
                    vm_bf[ti] = S.val(VM)

                def emit_e2mm(j):
                    S.wait("tensor", VM, vm_sqb[j])
                    if j >= 1:
                        S.wait("tensor", VM, vm_mult[j - 1])
                    elif sepr_free_sm:
                        S.wait("tensor", SM, sepr_free_sm)
                    TE(lambda t: t.matmul(psE[0:2, 0:512], e2_sb[:], sqb[:],
                                          start=True, stop=True), TM)
                    tm_e2[j] = S.val(TM)

                def emit_sqb(ti):
                    if ti >= 1:
                        S.wait("vector", TM, tm_e2[ti - 1])
                    V(lambda v, ti=ti: v.tensor_mul(sqb[:], rb[ti % 2][:],
                                                    rb[ti % 2][:]))
                    vm_sqb[ti] = S.val(VM)

                def emit_l2chain(j):
                    qk, _ = chunks[j]
                    hp = j % 2
                    lv = l2v[:, hp * 512:(hp + 1) * 512]
                    S.wait("vector", TM, tm_e2[j])
                    V(lambda v, lv=lv: v.tensor_scalar(lv, psE[0:2, 0:512],
                                                       1e-30, None, ALU.max))
                    vmv = S.val(VM)
                    S.wait("scalar", VM, vmv)
                    SC(lambda s, lv=lv: s.activation(lv, lv, AF.Ln,
                                                     bias=0.0, scale=1.0))
                    SC(lambda s, lv=lv: s.activation(lv, lv, AF.Exp,
                                                     bias=0.0, scale=-0.5))
                    smv = S.val(SM)
                    S.wait("vector", SM, smv)
                    V(lambda v, lv=lv: v.tensor_scalar(lv, lv, 1e8, None,
                                                       ALU.min))
                    if qk == 0:
                        V(lambda v, lv=lv: v.tensor_scalar(lv, lv, SCALE, None,
                                                           ALU.mult))
                    vm_l2[j] = S.val(VM)

                def emit_e2tmm(j):
                    hp = j % 2
                    lv = l2v[:, hp * 512:(hp + 1) * 512]
                    S.wait("tensor", VM, vm_l2[j])
                    TE(lambda t, lv=lv: t.matmul(psE[:, 0:512], e2t_sb[:], lv,
                                                 start=True, stop=True), TM)
                    tm_e2t[j] = S.val(TM)

                def emit_mult(j):
                    qk, s4 = chunks[j]
                    dst = qc_sb if qk == 0 else kc_sb
                    S.wait("vector", TM, tm_e2t[j])
                    V(lambda v, j=j, s4=s4, dst=dst: v.tensor_mul(
                        dst[:, s4 * 512:(s4 + 1) * 512], rb[j % 2][:],
                        psE[:, 0:512]))
                    vm_mult[j] = S.val(VM)

                if l == 0:
                    for ti in range(nch):
                        emit_proj(ti)
                        emit_evac_l0(ti)
                    vm_qk_pair = vm_bf[nch - 1]
                else:
                    for ti in range(nch + 1):
                        if ti < nch:
                            emit_proj(ti)
                            emit_raw(ti)
                        if ti >= 1:
                            emit_e2mm(ti - 1)
                        if ti < nch:
                            emit_sqb(ti)
                        if ti >= 1:
                            emit_l2chain(ti - 1)
                            emit_e2tmm(ti - 1)
                            emit_mult(ti - 1)
                    vm_qk_pair = vm_mult[nch - 1]
                qk_read_tm[par] = tm_proj[nch - 1]

                kc_lo = max(0, qlo_t - (L0_WIN - 1)) if l == 0 else 0

                def spans_for(kc):
                    ql = max(kc, qlo_t) * 128
                    qh = (min(kc + L0_WIN, qhi_t) if l == 0 else qhi_t) * 128
                    return [(sp, min(512, qh - sp)) for sp in range(ql, qh, 512)]

                def bank_subspans(sp, spw):
                    res, a = [], sp
                    base = qlo_t * 128
                    while a < sp + spw:
                        b = min(sp + spw, base + ((a - base) // 512 + 1) * 512)
                        res.append((a, b))
                        a = b
                    return res

                order = []
                for kc in range(kc_lo, qhi_t):
                    for (sp, spw) in spans_for(kc):
                        for which in ("E", "O", "S"):
                            for (a, b) in bank_subspans(sp, spw):
                                bank = (a - qlo_t * 128) // 512
                                key = (which if which != "E" else "C", bank)
                                order.append(key)
                first_ix, last_ix = {}, {}
                for ix, key in enumerate(order):
                    if key not in first_ix:
                        first_ix[key] = ix
                    last_ix[key] = ix
                oix = 0
                first_pv = True
                S.wait("tensor", VM, vm_qk_pair)
                for kc in range(kc_lo, qhi_t):
                    for (sp, spw) in spans_for(kc):
                        if sm_ab_free:
                            S.wait("tensor", SM, sm_ab_free)
                        for p01 in (0, 1):
                            TE(lambda t, kc=kc, p01=p01, sp=sp, spw=spw:
                               t.matmul((psA if p01 == 0 else psB)[:, 0:spw],
                                        kc_sb[p01 * 64:(p01 + 1) * 64,
                                              kc * 128:(kc + 1) * 128],
                                        qc_sb[p01 * 64:(p01 + 1) * 64,
                                              sp:sp + spw],
                                        start=True, stop=True,
                                        tile_position=(p01 * 64, 0)),
                               TM if p01 == 1 else None)
                        tm_sc = S.val(TM)
                        subs_ab = bank_subspans(sp, spw)
                        banks_here = sorted({(a - qlo_t * 128) // 512
                                             for (a, b) in subs_ab})
                        wv = max([tm_sc] + [last_pv_tm[b] for b in banks_here])
                        S.wait("scalar", TM, wv)
                        if l == 0:
                            for qi in range(sp // 128, (sp + spw) // 128):
                                off = qi * 128 - sp
                                qof = qi * 128 - qlo_t * 128
                                dlt = qi - kc
                                for p01 in (0, 1):
                                    src_ps = psA if p01 == 0 else psB
                                    dst = pb0 if p01 == 0 else pb1
                                    SC(lambda s, src_ps=src_ps, dst=dst,
                                             off=off, qof=qof, dlt=dlt:
                                       s.activation(dst[:, qof:qof + 128],
                                                    src_ps[:, off:off + 128],
                                                    AF.Exp,
                                                    bias=biasv_sb[:, dlt:dlt + 1],
                                                    scale=1.0))
                        else:
                            for (a, b) in subs_ab:
                                for p01 in (0, 1):
                                    src_ps = psA if p01 == 0 else psB
                                    dst = pb0 if p01 == 0 else pb1
                                    SC(lambda s, src_ps=src_ps, dst=dst, a=a,
                                             b=b, sp=sp, q0=qlo_t * 128:
                                       s.activation(dst[:, a - q0:b - q0],
                                                    src_ps[:, a - sp:b - sp],
                                                    AF.Exp,
                                                    bias=0.0, scale=1.0 / 1.5))
                        sm_exp = S.val(SM)
                        sm_ab_free = sm_exp
                        vm_tri = 0
                        if sp // 128 <= kc < (sp + spw + 127) // 128:
                            qof = kc * 128 - qlo_t * 128
                            S.wait("vector", SM, sm_exp)
                            V(lambda v, qof=qof: v.tensor_mul(
                                pb0[:, qof:qof + 128], pb0[:, qof:qof + 128],
                                tri_sb[:]))
                            V(lambda v, qof=qof: v.tensor_mul(
                                pb1[:, qof:qof + 128], pb1[:, qof:qof + 128],
                                tri_sb[:]))
                            vm_tri = S.val(VM)
                        newbanks = []
                        _o2 = oix
                        for wi in range(3):
                            for (a, b) in subs_ab:
                                if _o2 == first_ix[order[_o2]]:
                                    bnk = (a - qlo_t * 128) // 512
                                    if (bnk, a - qlo_t * 128,
                                            b - qlo_t * 128) not in newbanks:
                                        newbanks.append((bnk, a - qlo_t * 128,
                                                         b - qlo_t * 128))
                                _o2 += 1
                        vm_ms = 0
                        if newbanks:
                            wv = max(last_pv_tm[bnk] for (bnk, v0, v1) in newbanks)
                            if wv:
                                S.wait("vector", TM, wv)
                            for (bnk, v0, v1) in newbanks:
                                if v0 > bnk * 512:
                                    V(lambda v, bnk=bnk, v0=v0: v.memset(
                                        pb0[:, bnk * 512:v0], 0.0), None)
                                    V(lambda v, bnk=bnk, v0=v0: v.memset(
                                        pb1[:, bnk * 512:v0], 0.0), None)
                                if v1 < bnk * 512 + 512:
                                    V(lambda v, bnk=bnk, v1=v1: v.memset(
                                        pb0[:, v1:bnk * 512 + 512], 0.0), None)
                                    V(lambda v, bnk=bnk, v1=v1: v.memset(
                                        pb1[:, v1:bnk * 512 + 512], 0.0), VM)
                            vm_ms = S.val(VM)
                        S.wait("tensor", SM, sm_exp)
                        if vm_tri:
                            S.wait("tensor", VM, vm_tri)
                        if vm_ms:
                            S.wait("tensor", VM, vm_ms)
                        if first_pv:
                            S.wait("tensor", VM, pv_free_vm)
                            if sepr_free_sm:
                                S.wait("tensor", SM, sepr_free_sm)
                            first_pv = False
                        for wi, which in enumerate(("E", "O", "S")):
                            for ai, (a, b) in enumerate(subs_ab):
                                stf = (oix == first_ix[order[oix]])
                                spf = (oix == last_ix[order[oix]])
                                c0 = a - qlo_t * 128
                                c1 = b - qlo_t * 128
                                if stf:
                                    bnk = c0 // 512
                                    c0 = bnk * 512
                                    c1 = bnk * 512 + 512
                                lastmm = (wi == 2 and ai == len(subs_ab) - 1)
                                if which == "E":
                                    TE(lambda t, kc=kc, c0=c0, c1=c1,
                                             stf=stf, spf=spf, c=c:
                                       t.matmul(psC[0:65, c0:c1],
                                                vprime[:, kc * NH * 65 + (2 * c) * 65:
                                                       kc * NH * 65 + (2 * c) * 65 + 65],
                                                pb0[:, c0:c1], start=stf, stop=spf),
                                       None)
                                elif which == "O":
                                    TE(lambda t, kc=kc, c0=c0, c1=c1,
                                             stf=stf, spf=spf, c=c:
                                       t.matmul(psD[64:128, c0:c1],
                                                vprime[:, kc * NH * 65 + (2 * c + 1) * 65:
                                                       kc * NH * 65 + (2 * c + 1) * 65 + 64],
                                                pb1[:, c0:c1], start=stf, stop=spf,
                                                tile_position=(0, 64)),
                                       None)
                                else:
                                    bnk = c0 // 512
                                    TE(lambda t, c0=c0, c1=c1, bnk=bnk,
                                             stf=stf, spf=spf:
                                       t.matmul((psE if bnk == 0 else psR)
                                                [0:1, c0 - 512 * bnk:c1 - 512 * bnk],
                                                ones128b_sb[:, 0:1],
                                                pb1[:, c0:c1], start=stf, stop=spf),
                                       TM if lastmm else None)
                                oix += 1
                        tm_pv = S.val(TM)
                        for b in banks_here:
                            last_pv_tm[b] = tm_pv
                wv = max(last_pv_tm[0], last_pv_tm[1], rs_free_tm)
                S.wait("vector", TM, wv)
                V(lambda v: v.tensor_copy(rs_sb[64:65, 0:1024], psC[64:65, 0:1024]))
                V(lambda v: v.tensor_copy(rs_sb[0:1, 0:512], psE[0:1, 0:512]))
                V(lambda v: v.tensor_copy(rs_sb[0:1, 512:1024], psR[0:1, 0:512]))
                vm_rs = S.val(VM)
                V(lambda v: v.reciprocal(rs_sb[64:65, 0:1024],
                                         rs_sb[64:65, 0:1024]))
                V(lambda v: v.reciprocal(rs_sb[0:1, 0:1024],
                                         rs_sb[0:1, 0:1024]))
                sm_rr = S.val(SM)
                vm_rs = S.val(VM)
                vm_ofm = 0
                sm_rcp = 0
                for sp in range(2):
                    if sp == 0:
                        S.wait("tensor", SM, sm_rr)
                        S.wait("tensor", VM, vm_rs)
                    else:
                        S.wait("tensor", SM, sm_rcp)
                    TE(lambda t, sp=sp: t.matmul(
                        psE[:, 0:512], onesr_sb[64:65, 0:128],
                        rs_sb[64:65, sp * 512:(sp + 1) * 512],
                        start=True, stop=True, tile_position=(64, 0)), TM)
                    TE(lambda t, sp=sp: t.matmul(
                        psR[:, 0:512], ones1_sb[0:1, 0:128],
                        rs_sb[0:1, sp * 512:(sp + 1) * 512],
                        start=True, stop=True), TM)
                    tm_bc = S.val(TM)
                    S.wait("scalar", TM, tm_bc)
                    SC(lambda s: s.activation(rcp[0:64, 0:512], psE[0:64, 0:512],
                                              AF.Copy, bias=0.0, scale=1.0))
                    SC(lambda s: s.activation(rcp[64:128, 0:512],
                                              psR[64:128, 0:512],
                                              AF.Copy, bias=0.0, scale=1.0))
                    sm_rcp = S.val(SM)
                    S.wait("vector", SM, sm_rcp)
                    V(lambda v, sp=sp, c=c: v.tensor_mul(
                        ofm[0:64, c * 1024 + sp * 512:c * 1024 + (sp + 1) * 512],
                        psC[0:64, sp * 512:(sp + 1) * 512], rcp[0:64, 0:512]))
                    V(lambda v, sp=sp, c=c: v.tensor_mul(
                        ofm[64:128, c * 1024 + sp * 512:c * 1024 + (sp + 1) * 512],
                        psD[64:128, sp * 512:(sp + 1) * 512],
                        rcp[64:128, 0:512]))
                    vm_ofm = S.val(VM)
                    if sp == 1:
                        rs_free_tm = tm_bc
                        sepr_free_sm = sm_rcp
                pv_free_vm = vm_ofm
                last_pv_tm = [0, 0]
            S.wait("tensor", D_G, dg_wot)
            vm_woev = [0] * 8
            for tl in range(8):
                i = H * 8 + tl
                pp = psC if tl % 2 == 0 else psD
                if tl == 0:
                    S.wait("tensor", VM, pv_free_vm)
                if tl >= 2:
                    S.wait("tensor", VM, vm_woev[tl - 2])
                for n in range(2):
                    for k in range(DC):
                        TE(lambda t, k=k, n=n, tl=tl, pp=pp: t.matmul(
                            pp[:, n * 512:(n + 1) * 512],
                            ofm[:, k * 1024 + tl * 128:k * 1024 + (tl + 1) * 128],
                            wbig[:, k * D + n * 512:k * D + n * 512 + 512],
                            start=(k == 0), stop=(k == DC - 1)),
                           TM if (k == DC - 1 and n == 1) else None)
                tm_wo = S.val(TM)
                S.wait("vector", TM, tm_wo)
                V(lambda v, i=i: v.tensor_add(xt(i), xt(i), bobc[:]))
                scl = 0.5 if l == 1 else 1.0
                V(lambda v, i=i, scl=scl, pp=pp: v.scalar_tensor_tensor(
                    xt(i), pp[:, 0:1024], scl, xt(i), ALU.mult, ALU.add))
                vm_woev[tl] = S.val(VM)
            pv_free_vm = vm_woev[7]

    subs = []
    for l in range(2):
        subs += [(l, "Fa"), (l, "At"), (l, "Fb")]
    subs = subs[:n_sub]

    try:
      for seq in range(SEQS):
          if seq > 0:
              S.wait("sync", VM)
              S.dma("sync", y_d[seq - 1].rearrange("(i p) d -> p i d", p=128),
                    x_sb[:].rearrange("p (i d) -> p i d", d=D), ST)
              S.wait("sync", ST)
              S.dma("sync", x_sb[:].rearrange("p (i d) -> p i d", d=D),
                    x_d[seq].rearrange("(i p) d -> p i d", p=128), D_S)
          S.wait("vector", D_S)
          S.wait("scalar", D_S)
          S.wait("tensor", D_S)
          S.wait("vector", D_G)
          for (l, kind) in subs:
              with ExitStack() as pes:
                  if kind == "At":
                      attn_phase(l, seq, pes)
                  else:
                      ffn_phase(l, seq, 0 if kind == "Fa" else 1, pes)
                  S.wait("vector", TM)
                  S.wait("vector", SM)
                  S.wait("tensor", VM)
                  S.wait("tensor", SM)
                  S.wait("scalar", VM)
                  S.wait("scalar", TM)
                  S.wait("gpsimd", VM)
                  S.wait("sync", VM)
      S.wait("sync", VM)
      S.dma("sync", y_d[SEQS - 1].rearrange("(i p) d -> p i d", p=128),
              x_sb[:].rearrange("p (i d) -> p i d", d=D), ST)
      S.wait("sync", ST)
    except _StopEmit:
        pass

    with nc.Block() as block:
        S.build(block)
    return nc, es


def kernel(**inputs):
    host = _prep(inputs)
    x = np.ascontiguousarray(np.asarray(inputs["L0_attn_x"], dtype=np.float32))
    nc, es = build_program(N_SUBLAYERS, DEBUG_TAPS)
    in_maps = []
    for c in range(NCORES := 8):
        m = dict(host)
        m["x"] = np.ascontiguousarray(x[2 * c:2 * c + 2])
        in_maps.append(m)
    res = run_bass_kernel_spmd(nc, in_maps, core_ids=list(range(8)))
    es.close()
    return np.concatenate([np.asarray(res.results[c]["y"]) for c in range(8)],
                          axis=0).astype(np.float32)
